# version 18
# baseline (speedup 1.0000x reference)
import numpy as np
import ml_dtypes
import concourse.bass as bass
import concourse.mybir as mybir
from concourse.bass_utils import run_bass_kernel_spmd

F32 = mybir.dt.float32
F8 = mybir.dt.float8e4
BF16 = mybir.dt.bfloat16
AF = mybir.ActivationFunctionType
ALU = mybir.AluOpType
bfnp = ml_dtypes.bfloat16

T_TOK = 2048
DM = 1024
DI = 2048
N16 = 16
NT = 16
EPS_LN = 1e-5
EPS_RMS = float(np.finfo(np.float32).eps)
SQDI = float(np.sqrt(1.0 / 2048.0))
SQDM = float(np.sqrt(1.0 / 1024.0))

ENGS = ("sync", "gpsimd", "tensor", "vector", "scalar")
DBG_DUMP = False
SEMS = ("qs", "gs", "ts", "vs", "ss", "qc", "qx", "qy",
        "w1", "w2", "w3", "w4", "w5", "w6", "w7")


class Pass:

    def __init__(self, name, eng, sems):
        self.name = name
        self.eng = eng
        self.sems = sems
        self.c = {s: 0 for s in SEMS}

    def on(self, n):
        return self.name == n

    def do(self, eng_name, fn, inc=None):
        if self.name == eng_name:
            ins = fn(self.eng)
            if inc is not None:
                ins.then_inc(self.sems[inc], 1 if inc in ("ts", "vs", "ss") else 16)
        if inc is not None:
            self.c[inc] += 1 if inc in ("ts", "vs", "ss") else 16
            return self.c[inc]
        return None

    def wait(self, eng_name, sem, val=None):
        if val is None:
            val = self.c[sem]
        if val <= 0:
            return
        if self.name == eng_name:
            self.eng.wait_ge(self.sems[sem], val)


def build_nc():
    nc = bass.Bass("TRN2", target_bir_lowering=False)

    x_h = nc.dram_tensor("x", [T_TOK, DM], F32, kind="ExternalInput")
    win_h = nc.dram_tensor("w_in", [DM, 2 * DI], F32, kind="ExternalInput")
    wabc_h = nc.dram_tensor("w_abc", [DI, 48], F32, kind="ExternalInput")
    wout_h = nc.dram_tensor("w_out", [DI, DM], F32, kind="ExternalInput")
    nw_h = nc.dram_tensor("nw", [128, 16], F32, kind="ExternalInput")
    wr_h = nc.dram_tensor("w_rt", [DM, DM], F32, kind="ExternalInput")
    wk_h = nc.dram_tensor("w_kt", [DM, DM], F32, kind="ExternalInput")
    wv_h = nc.dram_tensor("w_vt", [DM, DM], F32, kind="ExternalInput")
    mur_h = nc.dram_tensor("mur", [128, 8], F32, kind="ExternalInput")
    muk_h = nc.dram_tensor("muk", [128, 8], F32, kind="ExternalInput")
    i128_h = nc.dram_tensor("i128", [128, 128], BF16, kind="ExternalInput")
    i128f_h = nc.dram_tensor("i128f", [128, 128], F32, kind="ExternalInput")
    i16_h = nc.dram_tensor("i16", [16, 16], BF16, kind="ExternalInput")
    mmask_h = nc.dram_tensor("mmask", [128, 128], BF16, kind="ExternalInput")
    y_h = nc.dram_tensor("y", [T_TOK, DM], F32, kind="ExternalOutput")
    if DBG_DUMP:
        dbg_h = nc.dram_tensor("dbg", [12, 128, 2048], F32, kind="ExternalOutput")

    from contextlib import ExitStack
    with ExitStack() as st:
        sb = lambda name, shape, dt: st.enter_context(nc.sbuf_tensor(name, shape, dt))
        psum = lambda name, shape, dt: st.enter_context(nc.psum_tensor(name, shape, dt))

        win = sb("win", [128, 8, 4096], BF16)
        wabc = sb("wabc", [128, 16, 48], BF16)
        wout = sb("wout", [128, 16, 1024], BF16)
        wr = sb("wr", [128, 8, 1024], BF16)
        wk = sb("wk", [128, 8, 1024], BF16)
        wv = sb("wv", [128, 8, 1024], BF16)
        nw = sb("nw_s", [128, 16], F32)
        i128 = sb("i128_s", [128, 128], BF16)
        i128f = sb("i128f_s", [128, 128], F32)
        i16 = sb("i16_s", [16, 16], BF16)
        mmask = sb("mmask_s", [128, 128], BF16)

        xt = [sb(f"xt{i}", [128, DM], F32) for i in range(3)]
        ln1b = [sb(f"ln1_{i}", [128, DM], BF16) for i in range(2)]
        ln1T = sb("ln1T", [128, DM], BF16)
        s1b = sb("s1b", [128, 512], BF16)
        s2b = sb("s2b", [128, 512], BF16)
        s1b2 = sb("s1b2", [128, 512], BF16)
        s2b2 = sb("s2b2", [128, 512], BF16)
        x_in = [sb(f"x_in{i}", [128, DI], BF16) for i in range(2)]
        x_inT = sb("x_inT", [128, DI], BF16)
        abc = sb("abc", [16, 3 * 128], F32)
        abc_tm = sb("abc_tm", [128, 48], F32)
        e16 = sb("e16", [16, 128], F32)
        lp = sb("lp", [16, 128], F32)
        exsc = sb("exsc", [16, 128], F32)
        exs2 = sb("exs2", [16, 128], F32)
        exs3 = sb("exs3", [16, 128], F32)
        exs4 = sb("exs4", [16, 64], F32)
        cint = sb("cint", [16, 128], BF16)
        cex2 = sb("cex2", [16, 64], BF16)
        bex1 = sb("bex1", [16, 64], BF16)
        bex1x = sb("bex1x", [16, 64], BF16)
        bex2 = sb("bex2", [16, 64], BF16)
        bsc = sb("bsc", [16, 128], BF16)
        bscT = sb("bscT", [128, 16], BF16)
        sv16 = sb("sv16", [16, 8], F32)
        dgl = sb("dgl", [16, 16], BF16)
        mt = sb("mt", [128, 128], BF16)
        state = sb("state", [16, DI], BF16)
        out_n = sb("out_n", [128, DI], BF16)
        out_nT = sb("out_nT", [128, DI], BF16)
        tscr = sb("tscr", [128, DM], F32)
        scrS = sb("scrS", [128, DM], F8)
        scrV = scrS
        stat = sb("stat", [128, 32], F32)
        cst = sb("cst", [128, 2], F32)
        ln2T = [sb(f"ln2T{i}", [128, 8, 130], BF16) for i in range(2)]
        mur = sb("mur_s", [128, 8], F32)
        muk = sb("muk_s", [128, 8], F32)
        dmix = mt
        xrT = ln1T
        xkT = x_inT
        r_b = x_in[0][:, 0:DM]
        kact = x_in[0][:, DM:2 * DM]
        kactT = out_nT
        fout = tscr

        py0 = psum("py0", [128, 1024], F32)
        py1 = psum("py1", [128, 1024], F32)
        po = psum("po", [128, 1024], F32)
        ptp = psum("ptp", [128, 512], BF16)
        pmisc = psum("pm", [128, 512], F32)

        sems = {s: st.enter_context(nc.semaphore(s)) for s in SEMS}
        blk = st.enter_context(nc.Block())


        def transpose_group(b, dst_sb, src_sb, cols, dt_groups=4):
            ngrp = cols // 512
            for g in range(ngrp):
                b.wait("tensor", "vs")
                for i in range(4):
                    k = g * 4 + i
                    b.do("tensor", lambda t, k=k, i=i: t.transpose(
                        ptp[:, i * 128:(i + 1) * 128],
                        src_sb[:, k * 128:(k + 1) * 128], i128[:]),
                        inc="ts" if i == 3 else None)
                b.wait("vector", "ts")
                b.do("vector", lambda v, g=g: v.tensor_copy(
                    dst_sb[:, g * 512:(g + 1) * 512], ptp[:]), inc="vs")

        def ln_stats_apply(b, x_ap, width, out_ap, c0, eps_ap, qs_val):
            inv = 1.0 / width
            b.wait("vector", "qx", qs_val)
            b.do("vector", lambda v: v.scalar_tensor_tensor(
                scrV[:, 0:width], x_ap, inv, x_ap, ALU.mult, ALU.mult,
                accum_out=stat[:, c0:c0 + 1]), inc="vs")
            b.do("vector", lambda v: v.tensor_copy(
                stat[:, 30:31], cst[:, 0:1]))
            b.wait("scalar", "qx", qs_val)
            b.wait("scalar", "vs")
            b.do("scalar", lambda s: s.activation(
                scrS[:, 0:width], x_ap, AF.Identity, scale=inv,
                accum_out=stat[:, c0 + 1:c0 + 2]))
            b.do("scalar", lambda sc: sc.activation(
                stat[:, 29:30], cst[:, 0:1], AF.Identity), inc="ss")
            b.wait("vector", "ss")
            b.do("vector", lambda v: v.scalar_tensor_tensor(
                stat[:, c0 + 2:c0 + 3], stat[:, c0 + 1:c0 + 2],
                stat[:, c0 + 1:c0 + 2], stat[:, c0:c0 + 1],
                ALU.mult, ALU.subtract), inc="vs")
            b.wait("scalar", "vs")
            b.do("scalar", lambda s: s.activation(
                stat[:, c0 + 3:c0 + 4], stat[:, c0 + 2:c0 + 3], AF.Ln,
                scale=-1.0, bias=eps_ap))
            b.do("scalar", lambda s: s.activation(
                stat[:, c0 + 3:c0 + 4], stat[:, c0 + 3:c0 + 4], AF.Exp,
                scale=-0.5), inc="ss")
            b.wait("vector", "ss")
            b.do("vector", lambda v: v.tensor_scalar(
                out_ap, x_ap, stat[:, c0 + 1:c0 + 2], stat[:, c0 + 3:c0 + 4],
                ALU.subtract, ALU.mult), inc="vs")

        EM5 = float(np.exp(-5.0))

        def sfence(b, inc="ss"):
            b.do("scalar", lambda sc: sc.activation(
                stat[:, 29:30], cst[:, 0:1], AF.Identity), inc=inc)

        def vfence(b, inc="vs"):
            b.do("vector", lambda v: v.tensor_copy(
                stat[:, 30:31], cst[:, 0:1]), inc=inc)

        def dump(b, slot, src_ap, pdim, fdim):
            if not DBG_DUMP:
                return
            b.wait("gpsimd", "vs")
            b.wait("gpsimd", "ss")
            b.wait("gpsimd", "ts")
            b.do("gpsimd", lambda g: g.dma_start(
                dbg_h[slot, 0:pdim, 0:fdim], src_ap), inc="gs")

        def run(b: Pass):
            qs_rec = {}
            for name, dst, src in [("i128", i128, i128_h), ("i128f", i128f, i128f_h),
                                   ("i16", i16, i16_h),
                                   ("mmask", mmask, mmask_h), ("nw", nw, nw_h),
                                   ("mur", mur, mur_h), ("muk", muk, muk_h)]:
                qs_rec[name] = b.do("sync", lambda q, d=dst, s=src:
                                    q.dma_start(d[:], s[:]), inc="qc")
            consts_qs = b.c["qc"]
            wsem = {"win": "w1", "wabc": "w2", "wout": "w3", "wr": "w4",
                    "wk": "w5", "wv": "w6"}
            for name, dst, src in [("win", win, win_h), ("wabc", wabc, wabc_h),
                                   ("wout", wout, wout_h), ("wr", wr, wr_h),
                                   ("wk", wk, wk_h), ("wv", wv, wv_h)]:
                b.do("gpsimd", lambda g, d=dst, s=src: g.dma_start(
                    d[:], s[:].rearrange("(k p) m -> p k m", p=128)),
                    inc=wsem[name])
            b.wait("vector", "w3", 16)
            b.wait("vector", "qc", consts_qs)
            for k in range(16):
                b.do("vector", lambda v, k=k: v.tensor_scalar(
                    wout[:, k, :], wout[:, k, :], nw[:, k:k + 1], None,
                    ALU.mult), inc="vs" if k == 15 else None)
            wout_ready_vs = b.c["vs"]

            b.do("vector", lambda v: v.memset(pmisc[:, 0:128], 0.0))
            b.do("vector", lambda v: v.memset(cst[:, 0:1], EPS_LN))
            b.do("vector", lambda v: v.memset(cst[:, 1:2], EPS_RMS))
            b.do("vector", lambda v: v.memset(state[:], 0.0))
            b.do("vector", lambda v: v.memset(ln2T[0][:], 0.0))
            b.do("vector", lambda v: v.memset(ln2T[1][:], 0.0), inc="vs")

            x_load_q = [0] * (2 * NT)
            x_store_q = [0] * (2 * NT)
            state_evac_vs = 0

            st_ln1T_ts = [0]
            lf_state = {"ln1_vs": [0, 0], "factors_vs": 0}

            def emit_lf(tF, tB):
                xbf = xt[tF % 3] if tF is not None else None
                c0 = 4 * (tF % 2) if tF is not None else 0
                if tF is not None and x_load_q[tF] == 0:
                    b.wait("sync", "vs")
                    x_load_q[tF] = b.do("sync", lambda q, tF=tF, xbf=xbf:
                        q.dma_start(xbf[:], x_h[tF * 128:(tF + 1) * 128, :]),
                        inc="qx")

                if tF is not None:
                    b.wait("vector", "qx", x_load_q[tF])
                    b.do("vector", lambda v: v.scalar_tensor_tensor(
                        scrV[:, 0:DM], xbf[:], 1.0 / DM, xbf[:], ALU.mult,
                        ALU.mult, accum_out=stat[:, c0:c0 + 1]), inc="vs")
                vfence(b)
                if tB is not None:
                    b.do("vector", lambda v: v.tensor_tensor_scan(
                        lp[:], e16[:], e16[:], 0.0, ALU.add, ALU.bypass))
                    vfence(b)
                    b.do("vector", lambda v: v.tensor_scalar(
                        sv16[:, 0:1], lp[:, 63:64], -1.0, None, ALU.mult))
                    b.do("vector", lambda v: v.tensor_scalar(
                        sv16[:, 1:2], lp[:, 127:128], -1.0, None, ALU.mult),
                        inc="vs")
                if tF is not None:
                    b.wait("scalar", "qx", x_load_q[tF])
                    b.wait("scalar", "vs")
                    b.do("scalar", lambda s: s.activation(
                        scrS[:, 0:DM], xbf[:], AF.Identity, scale=1.0 / DM,
                        accum_out=stat[:, c0 + 1:c0 + 2]))
                    sfence(b)
                    b.wait("vector", "ss")
                    b.do("vector", lambda v: v.scalar_tensor_tensor(
                        stat[:, c0 + 2:c0 + 3], stat[:, c0 + 1:c0 + 2], stat[:, c0 + 1:c0 + 2],
                        stat[:, c0:c0 + 1], ALU.mult, ALU.subtract), inc="vs")
                b.wait("scalar", "vs")
                if tF is not None:
                    b.do("scalar", lambda s: s.activation(
                        stat[:, c0 + 3:c0 + 4], stat[:, c0 + 2:c0 + 3], AF.Ln, scale=-1.0,
                        bias=cst[:, 0:1]))
                    b.do("scalar", lambda s: s.activation(
                        stat[:, c0 + 3:c0 + 4], stat[:, c0 + 3:c0 + 4], AF.Exp, scale=-0.5),
                        inc="ss" if tB is None else None)
                if tB is not None:
                    b.do("scalar", lambda s: s.activation(
                        exsc[:], lp[:], AF.Exp, scale=-1.0))
                    b.do("scalar", lambda s: s.activation(
                        exs2[:, 0:64], lp[:, 64:128], AF.Exp, scale=-1.0,
                        bias=lp[:, 63:64]))
                    b.do("scalar", lambda s: s.activation(
                        exs2[:, 64:128], lp[:, 0:64], AF.Exp))
                    b.do("scalar", lambda s: s.activation(
                        exs4[:, 0:64], lp[:, 64:128], AF.Exp,
                        bias=sv16[:, 0:1]))
                    b.do("scalar", lambda s: s.activation(
                        exs3[:], lp[:], AF.Exp, bias=sv16[:, 1:2]))
                    b.do("scalar", lambda s: s.activation(
                        sv16[:, 2:3], sv16[:, 0:1], AF.Exp))
                    b.do("scalar", lambda s: s.activation(
                        sv16[:, 3:4], sv16[:, 1:2], AF.Exp), inc="ss")
                b.wait("vector", "ss")
                if tF is not None:
                    b.wait("vector", "ts", st_ln1T_ts[0])
                    b.do("vector", lambda v, ln1=ln1b[tF % 2]: v.tensor_scalar(
                        ln1[:], xbf[:], stat[:, c0 + 1:c0 + 2], stat[:, c0 + 3:c0 + 4],
                        ALU.subtract, ALU.mult),
                        inc="vs" if tB is None else None)
                if tB is not None:
                    b.wait("vector", "qc", consts_qs)
                    B_ap = abc[:, 128:256]
                    C_ap = abc[:, 256:384]
                    b.do("vector", lambda v: v.tensor_tensor(
                        cint[:], exsc[:], C_ap, ALU.mult))
                    b.do("vector", lambda v: v.tensor_tensor(
                        cex2[:], exs2[:, 0:64], C_ap[:, 64:128], ALU.mult))
                    b.do("vector", lambda v: v.tensor_tensor(
                        bex1[:], exs2[:, 64:128], B_ap[:, 0:64], ALU.mult))
                    b.do("vector", lambda v: v.tensor_tensor(
                        bex2[:], exs4[:, 0:64], B_ap[:, 64:128], ALU.mult))
                    b.do("vector", lambda v: v.tensor_scalar(
                        bex1x[:], bex1[:], sv16[:, 2:3], None, ALU.mult))
                    b.do("vector", lambda v: v.tensor_tensor(
                        bsc[:], exs3[:], B_ap, ALU.mult))
                    b.do("vector", lambda v: v.tensor_scalar(
                        dgl[:], i16[:], sv16[:, 3:4], None, ALU.mult),
                        inc="vs")
                if False:
                    dump(b, 6, lp[:], 16, 128)
                    dump(b, 7, cint[:], 16, 128)
                    dump(b, 8, bex1[:], 16, 64)

                lf_state["factors_vs"] = b.c["vs"]
                if tF is not None:
                    lf_state["ln1_vs"][tF % 2] = b.c["vs"]

            emit_lf(0, None)
            emit_lf(1, None)
            for it in range(NT + 1):
                t = it
                tb = it - 1
                F = t < NT
                B = tb >= 0
                pf = t % 2
                pb = tb % 2
                xif = x_in[pf]
                xib = x_in[pb]
                xbf = xt[t % 3] if F else None
                xbb = xt[tb % 3] if B else None
                factors_vs = lf_state["factors_vs"]
                ln1_vs = lf_state["ln1_vs"][t % 2]

                if F:
                    b.wait("tensor", "qc", consts_qs)
                    transpose_group(b, ln1T, ln1b[t % 2], DM)
                    st_ln1T_ts[0] = b.c["ts"]
                ln1T_evac_vs = b.c["vs"]

                def xz_pass(jps):
                    b.wait("tensor", "vs")
                    b.wait("tensor", "ss")
                    b.wait("tensor", "w1", 16)
                    for k in range(8):
                        for half, pyb in ((0, py0), (1, py1)):
                            jp = jps[half]
                            b.do("tensor", lambda tt, k=k, jp=jp, pyb=pyb:
                                 tt.matmul(pyb[:, 0:512],
                                           ln1T[:, k * 128:(k + 1) * 128],
                                           win[:, k, jp * 512:(jp + 1) * 512],
                                           start=(k == 0), stop=(k == 7)))
                            b.do("tensor", lambda tt, k=k, jp=jp, pyb=pyb:
                                 tt.matmul(pyb[:, 512:1024],
                                           ln1T[:, k * 128:(k + 1) * 128],
                                           win[:, k, 2048 + jp * 512:
                                               2048 + (jp + 1) * 512],
                                           start=(k == 0), stop=(k == 7)),
                                 inc="ts" if (half == 1 and k == 7) else None)
                    b.wait("scalar", "ts")
                    for half, pyb, sa, sbuf_ in ((0, py0, s1b, s2b),
                                                 (1, py1, s1b2, s2b2)):
                        b.do("scalar", lambda s, pyb=pyb, sa=sa: s.activation(
                            sa[:], pyb[:, 0:512], AF.Sigmoid))
                        b.do("scalar", lambda s, pyb=pyb, sbuf_=sbuf_:
                             s.activation(sbuf_[:], pyb[:, 512:1024],
                                          AF.Sigmoid),
                             inc="ss" if half == 1 else None)
                    b.wait("vector", "ss")
                    for half, pyb, sa, sbuf_ in ((0, py0, s1b, s2b),
                                                 (1, py1, s1b2, s2b2)):
                        jp = jps[half]
                        b.do("vector", lambda v, sa=sa, sbuf_=sbuf_:
                             v.tensor_tensor(sa[:], sa[:], sbuf_[:], ALU.mult))
                        b.do("vector", lambda v, jp=jp, pyb=pyb, sa=sa:
                             v.tensor_tensor(xif[:, jp * 512:(jp + 1) * 512],
                                             pyb[:, 0:512], sa[:], ALU.mult),
                             inc="vs" if half == 1 else None)
                if F:
                    xz_pass((0, 1))

                if B:
                    b.wait("tensor", "vs", factors_vs)
                    b.wait("tensor", "vs", ln1T_evac_vs)
                    b.wait("tensor", "qc", consts_qs)
                    b.do("tensor", lambda tt: tt.matmul(
                        pmisc[0:64, 0:64], bex1[:], cint[:, 0:64],
                        start=True, stop=True))
                    b.do("tensor", lambda tt: tt.matmul(
                        pmisc[0:64, 64:128], bex1x[:], cex2[:],
                        start=True, stop=True))
                    b.do("tensor", lambda tt: tt.matmul(
                        pmisc[64:128, 64:128], bex2[:], cex2[:],
                        start=True, stop=True, tile_position=(0, 64)))
                    b.do("tensor", lambda tt: tt.transpose(
                        ptp[:, 0:16], bsc[:], i16[:]), inc="ts")
                    b.wait("vector", "ts")
                    b.wait("vector", "qc", consts_qs)
                    b.do("vector", lambda v: v.tensor_tensor(
                        mt[:], pmisc[:, 0:128], mmask[:], ALU.mult))
                    b.do("vector", lambda v: v.tensor_copy(
                        bscT[:], ptp[:, 0:16]), inc="vs")
                    mt_vs = b.c["vs"]
                if B and tb == 0:
                    dump(b, 9, mt[:], 128, 128)


                if B:
                    b.wait("tensor", "vs", mt_vs)
                    b.wait("tensor", "vs", state_evac_vs)
                    b.wait("tensor", "ss")
                    regs = [(po, 0), (po, 512), (pmisc, 0)]
                    ssq_cols = [8, 9, 10, 11]
                    out_evac_vs = {}
                    for dc in range(3):
                        reg, roff = regs[dc]
                        b.do("tensor", lambda tt, reg=reg, roff=roff, dc=dc:
                             tt.matmul(reg[:, roff:roff + 512], cint[:],
                                       state[:, dc * 512:(dc + 1) * 512],
                                       start=True, stop=False))
                        b.do("tensor", lambda tt, reg=reg, roff=roff, dc=dc:
                             tt.matmul(reg[:, roff:roff + 512], mt[:],
                                       xib[:, dc * 512:(dc + 1) * 512],
                                       start=False, stop=True), inc="ts")
                        b.wait("vector", "ts")
                        out_evac_vs[dc] = b.do("vector",
                            lambda v, reg=reg, roff=roff, dc=dc: v.tensor_copy(
                                out_n[:, dc * 512:(dc + 1) * 512],
                                reg[:, roff:roff + 512]), inc="vs")
                        b.wait("scalar", "vs", out_evac_vs[dc])
                        b.do("scalar", lambda s, dc=dc, c=ssq_cols[dc]:
                             s.activation(scrS[:, 0:512],
                                          out_n[:, dc * 512:(dc + 1) * 512],
                                          AF.Square, scale=SQDI,
                                          accum_out=stat[:, c:c + 1]),
                             inc="ss")

                if F:
                    xz_pass((2, 3))
                    gating_vs = b.c["vs"]

                if B:
                    b.wait("tensor", "vs", out_evac_vs[0])
                    b.do("tensor", lambda tt: tt.matmul(
                        po[:, 0:512], cint[:], state[:, 1536:2048],
                        start=True, stop=False))
                    b.do("tensor", lambda tt: tt.matmul(
                        po[:, 0:512], mt[:], xib[:, 1536:2048],
                        start=False, stop=True), inc="ts")
                    b.wait("vector", "ts")
                    out_evac_vs[3] = b.do("vector", lambda v: v.tensor_copy(
                        out_n[:, 1536:2048], po[:, 0:512]), inc="vs")
                    b.wait("scalar", "vs", out_evac_vs[3])
                    b.do("scalar", lambda s: s.activation(
                        scrS[:, 0:512], out_n[:, 1536:2048], AF.Square,
                        scale=SQDI, accum_out=stat[:, 11:12]))
                    sfence(b)

                if B:
                    b.wait("tensor", "ss")
                    sregs = [(pmisc, 0, out_evac_vs[2]),
                             (po, 512, out_evac_vs[1]),
                             (po, 0, out_evac_vs[3])]
                    st_evac = {}
                    for dc in range(3):
                        reg, roff, wvv = sregs[dc]
                        b.wait("tensor", "vs", wvv)
                        b.do("tensor", lambda tt, reg=reg, roff=roff, dc=dc:
                             tt.matmul(reg[0:16, roff:roff + 512], dgl[:],
                                       state[:, dc * 512:(dc + 1) * 512],
                                       start=True, stop=False))
                        b.do("tensor", lambda tt, reg=reg, roff=roff, dc=dc:
                             tt.matmul(reg[0:16, roff:roff + 512], bscT[:],
                                       xib[:, dc * 512:(dc + 1) * 512],
                                       start=False, stop=True), inc="ts")
                        b.wait("vector", "ts")
                        st_evac[dc] = b.do("vector",
                            lambda v, reg=reg, roff=roff, dc=dc: v.tensor_copy(
                                state[:, dc * 512:(dc + 1) * 512],
                                reg[0:16, roff:roff + 512]), inc="vs")

                if F:
                    transpose_group(b, x_inT, xif, DI)
                    x_inT_vs = b.c["vs"]
                if F and t == 0:
                    dump(b, 1, xif[:], 128, DI)
                    dump(b, 2, x_inT[:], 128, DI)

                if B:
                    b.wait("tensor", "vs", st_evac[0])
                    b.do("tensor", lambda tt: tt.matmul(
                        pmisc[0:16, 0:512], dgl[:], state[:, 1536:2048],
                        start=True, stop=False))
                    b.do("tensor", lambda tt: tt.matmul(
                        pmisc[0:16, 0:512], bscT[:], xib[:, 1536:2048],
                        start=False, stop=True), inc="ts")
                    b.wait("vector", "ts")
                    st_evac[3] = b.do("vector", lambda v: v.tensor_copy(
                        state[:, 1536:2048], pmisc[0:16, 0:512]), inc="vs")
                    state_evac_vs = st_evac[3]

                if B:
                    b.wait("vector", "ss")
                    b.do("vector", lambda v: v.tensor_reduce(
                        stat[:, 24:25], stat[:, 8:12], mybir.AxisListType.X,
                        ALU.add), inc="vs")
                    b.wait("scalar", "vs")
                    b.do("scalar", lambda s: s.activation(
                        stat[:, 25:26], stat[:, 24:25], AF.Ln,
                        scale=1.0, bias=cst[:, 1:2]))
                    b.do("scalar", lambda s: s.activation(
                        stat[:, 25:26], stat[:, 25:26], AF.Exp, scale=-0.5),
                        inc="ss")
                    if DBG_DUMP and tb == 0:
                        b.wait("gpsimd", "vs")
                        b.do("gpsimd", lambda g: g.dma_start(
                            dbg_h[2, :, 0:DI], out_n[:]), inc="gs")
                        b.wait("vector", "gs")
                    b.wait("vector", "ss")
                    b.do("vector", lambda v: v.tensor_scalar(
                        out_n[:], out_n[:], stat[:, 25:26], None, ALU.mult),
                        inc="vs")
                if B and tb == 0:
                    dump(b, 10, out_n[:], 128, DI)
                    dump(b, 11, stat[:], 128, 32)

                if F:
                    b.wait("tensor", "vs", x_inT_vs)
                    if B:
                        b.wait("tensor", "vs", st_evac[3])
                    b.wait("tensor", "w2", 16)
                    for k in range(16):
                        b.do("tensor", lambda tt, k=k: tt.matmul(
                            pmisc[:, 0:48], x_inT[:, k * 128:(k + 1) * 128],
                            wabc[:, k, :], start=(k == 0), stop=(k == 15)),
                            inc="ts" if k == 15 else None)
                    b.wait("vector", "ts")
                    b.do("vector", lambda v: v.tensor_copy(
                        abc_tm[:], pmisc[:, 0:48]), inc="vs")
                    abc_tm_vs = b.c["vs"]
                if F and t == 0:
                    dump(b, 3, abc_tm[:], 128, 48)

                if B:
                    transpose_group(b, out_nT, out_n, DI)
                    out_nT_vs = b.c["vs"]

                if F:
                    b.wait("tensor", "vs", abc_tm_vs)
                    b.wait("tensor", "qc", consts_qs)
                    for j in range(3):
                        b.do("tensor", lambda tt, j=j: tt.transpose(
                            pmisc[0:16, 128 + j * 128:256 + j * 128],
                            abc_tm[:, j * 16:(j + 1) * 16], i128f[:]),
                            inc="ts" if j == 2 else None)
                    b.wait("scalar", "ts")
                    b.do("scalar", lambda s: s.activation(
                        e16[:], pmisc[0:16, 128:256], AF.Exp), inc="ss")
                    b.wait("vector", "ts")
                    b.wait("vector", "ss")
                    b.do("vector", lambda v: v.tensor_copy(
                        abc[:, 128:256], pmisc[0:16, 256:384]))
                    b.do("vector", lambda v: v.tensor_copy(
                        abc[:, 256:384], pmisc[0:16, 384:512]))
                    b.do("vector", lambda v: v.tensor_scalar(
                        e16[:], e16[:], EM5, 1.0, ALU.max, ALU.min),
                        inc="vs")
                if F and t == 0:
                    dump(b, 4, e16[:], 16, 128)
                    dump(b, 5, abc[:, 128:384], 16, 256)

                if B:
                    b.wait("tensor", "vs", out_nT_vs)
                    b.wait("tensor", "vs", wout_ready_vs)
                    for k in range(16):
                        for mc in range(2):
                            b.do("tensor", lambda tt, mc=mc, k=k: tt.matmul(
                                po[:, mc * 512:(mc + 1) * 512],
                                out_nT[:, k * 128:(k + 1) * 128],
                                wout[:, k, mc * 512:(mc + 1) * 512],
                                start=(k == 0), stop=(k == 15)),
                                inc="ts" if (mc == 1 and k == 15) else None)
                    b.wait("scalar", "ts")
                    b.do("scalar", lambda s: s.activation(
                        scrS[:, 0:1024], po[:], AF.Square, scale=SQDM,
                        accum_out=stat[:, 12:13]))
                    b.do("scalar", lambda s: s.activation(
                        scrS[:, 0:1024], po[:], AF.Identity, scale=1.0 / DM,
                        accum_out=stat[:, 13:14]))
                    sfence(b)
                    b.wait("vector", "ss")
                    b.do("vector", lambda v: v.scalar_tensor_tensor(
                        stat[:, 14:15], stat[:, 13:14], stat[:, 13:14],
                        stat[:, 12:13], ALU.mult, ALU.subtract), inc="vs")
                    b.wait("scalar", "vs")
                    b.do("scalar", lambda s: s.activation(
                        stat[:, 15:16], stat[:, 14:15], AF.Ln, scale=-1.0,
                        bias=cst[:, 0:1]))
                    b.do("scalar", lambda s: s.activation(
                        stat[:, 15:16], stat[:, 15:16], AF.Exp, scale=-0.5),
                        inc="ss")
                    b.wait("vector", "ss")
                    b.wait("vector", "qy", x_store_q[tb - 1] if tb >= 1 else 0)
                    b.do("vector", lambda v: v.tensor_scalar(
                        tscr[:], po[:], stat[:, 13:14], stat[:, 15:16],
                        ALU.subtract, ALU.mult))
                    b.do("vector", lambda v, xbb=xbb: v.tensor_tensor(
                        tscr[:], tscr[:], xbb[:], ALU.add), inc="vs")
                    b.wait("sync", "vs")
                    x_store_q[tb] = b.do("sync", lambda q, tb=tb:
                        q.dma_start(y_h[tb * 128:(tb + 1) * 128, :], tscr[:]),
                        inc="qy")

                emit_lf(t + 2 if t + 2 < NT else None,
                        t if t < NT else None)

            phaseA_done_q = b.c["qy"]

            for it in range(NT + 1):
                u = it
                ub = it - 1
                F = u < NT
                B = ub >= 0
                xbf = xt[u % 3] if F else None
                xbb = xt[ub % 3] if B else None

                if F:
                    b.wait("sync", "qy", phaseA_done_q)
                    if u >= 2:
                        b.wait("sync", "qy", x_store_q[NT + u - 2])
                    x_load_q[NT + u] = b.do("sync", lambda q, u=u, xbf=xbf:
                        q.dma_start(xbf[:], y_h[u * 128:(u + 1) * 128, :]),
                        inc="qx")

                if B:
                    b.wait("tensor", "vs")
                    b.wait("tensor", "ss")
                    b.wait("tensor", "w4", 16)
                    b.wait("tensor", "w5", 16)
                    for k in range(8):
                        for mc in range(2):
                            b.do("tensor", lambda tt, mc=mc, k=k: tt.matmul(
                                py0[:, mc * 512:(mc + 1) * 512],
                                xrT[:, k * 128:(k + 1) * 128],
                                wr[:, k, mc * 512:(mc + 1) * 512],
                                start=(k == 0), stop=(k == 7)))
                    for k in range(8):
                        for mc in range(2):
                            b.do("tensor", lambda tt, mc=mc, k=k: tt.matmul(
                                py1[:, mc * 512:(mc + 1) * 512],
                                xkT[:, k * 128:(k + 1) * 128],
                                wk[:, k, mc * 512:(mc + 1) * 512],
                                start=(k == 0), stop=(k == 7)),
                                inc="ts" if (mc == 1 and k == 7) else None)
                    rk_ts = b.c["ts"]

                ln2 = ln1b[u % 2] if F else None
                if F:
                    ln_stats_apply(b, xbf[:], DM, ln2[:], 0, cst[:, 0:1],
                                   x_load_q[NT + u])

                if B:
                    b.wait("scalar", "ts", rk_ts)
                    b.do("scalar", lambda s: s.activation(
                        r_b, py0[:], AF.Sigmoid), inc="ss")
                    b.wait("vector", "ts", rk_ts)
                    b.do("vector", lambda v: v.tensor_scalar(
                        tscr[:], py1[:], 0.0, None, ALU.max))
                    b.do("vector", lambda v: v.tensor_tensor(
                        kact, tscr[:], tscr[:], ALU.mult), inc="vs")
                    kact_vs = b.c["vs"]

                if F:
                    par = u % 2
                    lnb = ln2T[par]
                    lnprev = ln2T[1 - par]
                    for g in range(2):
                        b.wait("tensor", "vs")
                        b.wait("tensor", "qc", consts_qs)
                        for i in range(4):
                            k = g * 4 + i
                            b.do("tensor", lambda tt, k=k, i=i: tt.transpose(
                                ptp[:, i * 128:(i + 1) * 128],
                                ln2[:, k * 128:(k + 1) * 128], i128[:]),
                                inc="ts" if i == 3 else None)
                        b.wait("vector", "ts")
                        for i in range(4):
                            k = g * 4 + i
                            b.do("vector", lambda v, k=k, i=i: v.tensor_copy(
                                lnb[:, k, 1:129],
                                ptp[:, i * 128:(i + 1) * 128]),
                                inc="vs" if i == 3 else None)
                    if u > 0:
                        for k in range(8):
                            b.do("vector", lambda v, k=k: v.tensor_copy(
                                lnb[:, k, 0:1], lnprev[:, k, 128:129]),
                                inc="vs" if k == 7 else None)
                    b.wait("vector", "qc", consts_qs)
                    for k in range(8):
                        b.do("vector", lambda v, k=k: v.tensor_tensor(
                            dmix[:], lnb[:, k, 1:129], lnb[:, k, 0:128],
                            ALU.subtract))
                        b.do("vector", lambda v, k=k: v.scalar_tensor_tensor(
                            xrT[:, k * 128:(k + 1) * 128], dmix[:],
                            mur[:, k:k + 1], lnb[:, k, 0:128],
                            ALU.mult, ALU.add))
                        b.do("vector", lambda v, k=k: v.scalar_tensor_tensor(
                            xkT[:, k * 128:(k + 1) * 128], dmix[:],
                            muk[:, k:k + 1], lnb[:, k, 0:128],
                            ALU.mult, ALU.add), inc="vs" if k == 7 else None)

                if B:
                    b.wait("tensor", "vs", kact_vs)
                    transpose_group(b, kactT, kact, DM)
                    kactT_vs = b.c["vs"]
                    b.wait("tensor", "vs", kactT_vs)
                    b.wait("tensor", "ss")
                    b.wait("tensor", "w6", 16)
                    for k in range(8):
                        for mc in range(2):
                            b.do("tensor", lambda tt, mc=mc, k=k: tt.matmul(
                                po[:, mc * 512:(mc + 1) * 512],
                                kactT[:, k * 128:(k + 1) * 128],
                                wv[:, k, mc * 512:(mc + 1) * 512],
                                start=(k == 0), stop=(k == 7)),
                                inc="ts" if (mc == 1 and k == 7) else None)
                    b.wait("vector", "ts")
                    b.wait("vector", "qy",
                           x_store_q[NT + ub - 1] if ub >= 1 else phaseA_done_q)
                    b.do("vector", lambda v: v.tensor_tensor(
                        fout[:], po[:], r_b, ALU.mult), inc="vs")
                    if DBG_DUMP and ub == 0:
                        dump(b, 2, xbb[:], 128, DM)
                        dump(b, 0, fout[:], 128, DM)
                        dump(b, 1, r_b, 128, DM)
                        dump(b, 4, kact, 128, DM)
                        dump(b, 5, xrT[:, 0:DM], 128, DM)
                        dump(b, 6, xkT[:, 0:DM], 128, DM)
                        dump(b, 7, ln2[:], 128, DM)
                    b.do("vector", lambda v: v.scalar_tensor_tensor(
                        scrV[:, 0:1024], fout[:], 1.0 / DM, fout[:],
                        ALU.mult, ALU.mult, accum_out=stat[:, 16:17]))
                    vfence(b)
                    b.wait("scalar", "vs")
                    b.do("scalar", lambda s: s.activation(
                        scrS[:, 0:1024], fout[:], AF.Identity,
                        scale=1.0 / DM, accum_out=stat[:, 17:18]))
                    sfence(b)
                    b.wait("vector", "ss")
                    b.do("vector", lambda v: v.scalar_tensor_tensor(
                        stat[:, 18:19], stat[:, 17:18], stat[:, 17:18],
                        stat[:, 16:17], ALU.mult, ALU.subtract), inc="vs")
                    b.wait("scalar", "vs")
                    b.do("scalar", lambda s: s.activation(
                        stat[:, 19:20], stat[:, 18:19], AF.Ln, scale=-1.0,
                        bias=cst[:, 0:1]))
                    b.do("scalar", lambda s: s.activation(
                        stat[:, 19:20], stat[:, 19:20], AF.Exp, scale=-0.5),
                        inc="ss")
                    b.wait("vector", "ss")
                    b.do("vector", lambda v: v.tensor_scalar(
                        fout[:], fout[:], stat[:, 17:18], stat[:, 19:20],
                        ALU.subtract, ALU.mult))
                    b.do("vector", lambda v, xbb=xbb: v.tensor_tensor(
                        xbb[:], fout[:], xbb[:], ALU.add), inc="vs")
                    if DBG_DUMP and ub == 0:
                        dump(b, 8, stat[:], 128, 32)
                        dump(b, 3, xbb[:], 128, DM)
                    b.wait("sync", "vs")
                    x_store_q[NT + ub] = b.do("sync", lambda q, ub=ub, xbb=xbb:
                        q.dma_start(y_h[ub * 128:(ub + 1) * 128, :], xbb[:]),
                        inc="qy")

            b.wait("sync", "qx", b.c["qx"])
            b.wait("sync", "qy", b.c["qy"])
            for _wn in ("w1", "w2", "w3", "w4", "w5", "w6"):
                b.wait("gpsimd", _wn, 16)

        @blk.sync
        def _(sync):
            run(Pass("sync", sync, sems))

        @blk.gpsimd
        def _(g):
            run(Pass("gpsimd", g, sems))

        @blk.tensor
        def _(tt):
            run(Pass("tensor", tt, sems))

        @blk.vector
        def _(v):
            run(Pass("vector", v, sems))

        @blk.scalar
        def _(s):
            run(Pass("scalar", s, sems))

    return nc


_NC_CACHE = None


def kernel(L0_attn_x, L0_attn_W_in, L0_attn_W_A, L0_attn_W_B, L0_attn_W_C,
           L0_attn_W_out, L0_attn_norm_w, L0_ffn_W_r, L0_ffn_W_k, L0_ffn_W_v,
           L0_ffn_mu_r, L0_ffn_mu_k):
    global _NC_CACHE
    x = np.ascontiguousarray(np.asarray(L0_attn_x, dtype=np.float32))
    w_in = np.ascontiguousarray(np.asarray(L0_attn_W_in, dtype=np.float32))
    w_abc = np.ascontiguousarray(np.concatenate(
        [np.asarray(L0_attn_W_A, dtype=np.float32),
         np.asarray(L0_attn_W_B, dtype=np.float32),
         np.asarray(L0_attn_W_C, dtype=np.float32)], axis=1))
    w_out = np.ascontiguousarray(np.asarray(L0_attn_W_out, dtype=np.float32))
    nw = np.ascontiguousarray(
        np.asarray(L0_attn_norm_w, dtype=np.float32).reshape(16, 128).T)
    w_rt = np.ascontiguousarray(np.asarray(L0_ffn_W_r, dtype=np.float32).T)
    w_kt = np.ascontiguousarray(np.asarray(L0_ffn_W_k, dtype=np.float32).T)
    w_vt = np.ascontiguousarray(np.asarray(L0_ffn_W_v, dtype=np.float32).T)
    mur = np.ascontiguousarray(
        np.asarray(L0_ffn_mu_r, dtype=np.float32).reshape(8, 128).T)
    muk = np.ascontiguousarray(
        np.asarray(L0_ffn_mu_k, dtype=np.float32).reshape(8, 128).T)
    i128 = np.eye(128, dtype=np.float32).astype(bfnp)
    i128f = np.eye(128, dtype=np.float32)
    i16 = np.eye(16, dtype=np.float32).astype(bfnp)
    H = 64
    mm = np.zeros((128, 128), np.float32)
    mm[:H, :H] = np.triu(np.ones((H, H)))
    mm[H:, H:] = np.triu(np.ones((H, H)))
    mm[:H, H:] = 1.0
    mmask = mm.astype(bfnp)

    if _NC_CACHE is None:
        _NC_CACHE = build_nc()
    nc = _NC_CACHE

    shared = dict(w_in=w_in, w_abc=w_abc, w_out=w_out, nw=nw, w_rt=w_rt,
                  w_kt=w_kt, w_vt=w_vt, mur=mur, muk=muk, i128=i128, i128f=i128f,
                  i16=i16, mmask=mmask)
    in_maps = [dict(x=np.ascontiguousarray(x[c]), **shared) for c in range(8)]
    res = run_bass_kernel_spmd(nc, in_maps, core_ids=list(range(8)))
    out = np.stack([np.asarray(r["y"]) for r in res.results], axis=0)
    return out.astype(np.float32)


# version 20
# speedup vs baseline: 1.0908x; 1.0908x over previous
import numpy as np
import ml_dtypes
import concourse.bass as bass
import concourse.mybir as mybir
from concourse.bass_utils import run_bass_kernel_spmd

F32 = mybir.dt.float32
F8 = mybir.dt.float8e4
BF16 = mybir.dt.bfloat16
AF = mybir.ActivationFunctionType
ALU = mybir.AluOpType
bfnp = ml_dtypes.bfloat16

T_TOK = 2048
DM = 1024
DI = 2048
N16 = 16
NT = 16
EPS_LN = 1e-5
EPS_RMS = float(np.finfo(np.float32).eps)
SQDI = float(np.sqrt(1.0 / 2048.0))
SQDM = float(np.sqrt(1.0 / 1024.0))

ENGS = ("sync", "gpsimd", "tensor", "vector", "scalar")
DBG_DUMP = False
SEMS = ("qs", "gs", "ts", "vs", "ss", "qc", "qx", "qy",
        "w1", "w2", "w3", "w4", "w5", "w6", "w7")


class Pass:

    def __init__(self, name, eng, sems):
        self.name = name
        self.eng = eng
        self.sems = sems
        self.c = {s: 0 for s in SEMS}

    def on(self, n):
        return self.name == n

    def do(self, eng_name, fn, inc=None):
        if self.name == eng_name:
            ins = fn(self.eng)
            if inc is not None:
                ins.then_inc(self.sems[inc], 1 if inc in ("ts", "vs", "ss") else 16)
        if inc is not None:
            self.c[inc] += 1 if inc in ("ts", "vs", "ss") else 16
            return self.c[inc]
        return None

    def wait(self, eng_name, sem, val=None):
        if val is None:
            val = self.c[sem]
        if val <= 0:
            return
        if self.name == eng_name:
            self.eng.wait_ge(self.sems[sem], val)


def build_nc():
    nc = bass.Bass("TRN2", target_bir_lowering=False)

    x_h = nc.dram_tensor("x", [T_TOK, DM], F32, kind="ExternalInput")
    win_h = nc.dram_tensor("w_in", [DM, 2 * DI], F32, kind="ExternalInput")
    wabc_h = nc.dram_tensor("w_abc", [DI, 48], F32, kind="ExternalInput")
    wout_h = nc.dram_tensor("w_out", [DI, DM], F32, kind="ExternalInput")
    nw_h = nc.dram_tensor("nw", [128, 16], F32, kind="ExternalInput")
    wr_h = nc.dram_tensor("w_rt", [DM, DM], F32, kind="ExternalInput")
    wk_h = nc.dram_tensor("w_kt", [DM, DM], F32, kind="ExternalInput")
    wv_h = nc.dram_tensor("w_vt", [DM, DM], F32, kind="ExternalInput")
    mur_h = nc.dram_tensor("mur", [128, 8], F32, kind="ExternalInput")
    muk_h = nc.dram_tensor("muk", [128, 8], F32, kind="ExternalInput")
    i128_h = nc.dram_tensor("i128", [128, 128], BF16, kind="ExternalInput")
    i128f_h = nc.dram_tensor("i128f", [128, 128], F32, kind="ExternalInput")
    i16_h = nc.dram_tensor("i16", [16, 16], BF16, kind="ExternalInput")
    mmask_h = nc.dram_tensor("mmask", [128, 128], BF16, kind="ExternalInput")
    y_h = nc.dram_tensor("y", [T_TOK, DM], F32, kind="ExternalOutput")
    if DBG_DUMP:
        dbg_h = nc.dram_tensor("dbg", [12, 128, 2048], F32, kind="ExternalOutput")

    from contextlib import ExitStack
    with ExitStack() as st:
        sb = lambda name, shape, dt: st.enter_context(nc.sbuf_tensor(name, shape, dt))
        psum = lambda name, shape, dt: st.enter_context(nc.psum_tensor(name, shape, dt))

        win = sb("win", [128, 8, 4096], BF16)
        wabc = sb("wabc", [128, 16, 48], BF16)
        wout = sb("wout", [128, 16, 1024], BF16)
        wr = sb("wr", [128, 8, 1024], BF16)
        wk = sb("wk", [128, 8, 1024], BF16)
        wv = sb("wv", [128, 8, 1024], BF16)
        nw = sb("nw_s", [128, 16], F32)
        i128 = sb("i128_s", [128, 128], BF16)
        i128f = sb("i128f_s", [128, 128], F32)
        i16 = sb("i16_s", [16, 16], BF16)
        mmask = sb("mmask_s", [128, 128], BF16)

        xt = [sb(f"xt{i}", [128, DM], F32) for i in range(3)]
        ln1b = [sb(f"ln1_{i}", [128, DM], BF16) for i in range(2)]
        ln1T = sb("ln1T", [128, DM], BF16)
        s1b = sb("s1b", [128, 512], BF16)
        s2b = sb("s2b", [128, 512], BF16)
        s1b2 = sb("s1b2", [128, 512], BF16)
        s2b2 = sb("s2b2", [128, 512], BF16)
        x_in = [sb(f"x_in{i}", [128, DI], BF16) for i in range(2)]
        x_inT = sb("x_inT", [128, DI], BF16)
        abc = sb("abc", [16, 3 * 128], F32)
        abc_tm = sb("abc_tm", [128, 48], F32)
        e16 = sb("e16", [16, 128], F32)
        lp = sb("lp", [16, 128], F32)
        exsc = sb("exsc", [16, 128], F32)
        exs2 = sb("exs2", [16, 128], F32)
        exs3 = sb("exs3", [16, 128], F32)
        exs4 = sb("exs4", [16, 64], F32)
        cint = sb("cint", [16, 128], BF16)
        cex2 = sb("cex2", [16, 64], BF16)
        bex1 = sb("bex1", [16, 64], BF16)
        bex1x = sb("bex1x", [16, 64], BF16)
        bex2 = sb("bex2", [16, 64], BF16)
        bsc = sb("bsc", [16, 128], BF16)
        bscT = sb("bscT", [128, 16], BF16)
        sv16 = sb("sv16", [16, 8], F32)
        dgl = sb("dgl", [16, 16], BF16)
        mt = sb("mt", [128, 128], BF16)
        state = sb("state", [16, DI], BF16)
        out_n = sb("out_n", [128, DI], BF16)
        out_nT = sb("out_nT", [128, DI], BF16)
        tscr = sb("tscr", [128, DM], F32)
        scrS = sb("scrS", [128, DM], F8)
        scrV = scrS
        stat = sb("stat", [128, 32], F32)
        cst = sb("cst", [128, 2], F32)
        ln2T = [sb(f"ln2T{i}", [128, 8, 130], BF16) for i in range(2)]
        mur = sb("mur_s", [128, 8], F32)
        muk = sb("muk_s", [128, 8], F32)
        dmix = mt
        xrT = ln1T
        xkT = x_inT
        r_b = x_in[0][:, 0:DM]
        kact = x_in[0][:, DM:2 * DM]
        kactT = out_nT
        fout = tscr

        py0 = psum("py0", [128, 1024], F32)
        py1 = psum("py1", [128, 1024], F32)
        po = psum("po", [128, 1024], F32)
        ptp = psum("ptp", [128, 512], BF16)
        pmisc = psum("pm", [128, 512], F32)

        sems = {s: st.enter_context(nc.semaphore(s)) for s in SEMS}
        blk = st.enter_context(nc.Block())


        rec = {}

        def rec_init():
            rec.clear()
            rec.update({"ptp_vs": 0, "ln1T_vs": 0, "gate_vs": 0, "gate_ss": 0,
                        "mt_vs": 0, "abc_tm_vs": 0, "ez_ss": 0, "bc_vs": 0,
                        "postln_vs": 0, "postln_ss": 0, "rms_vs": 0,
                        "mix_vs": 0, "rb_ss": 0, "kact_vs": 0, "fout_vs": 0,
                        "ln2_vs": 0})

        def transpose_group(b, dst_sb, src_sb, cols, src_vs):
            ngrp = cols // 512
            for g in range(ngrp):
                b.wait("tensor", "vs", max(src_vs, rec["ptp_vs"]))
                for i in range(4):
                    k = g * 4 + i
                    b.do("tensor", lambda t, k=k, i=i: t.transpose(
                        ptp[:, i * 128:(i + 1) * 128],
                        src_sb[:, k * 128:(k + 1) * 128], i128[:]),
                        inc="ts" if i == 3 else None)
                b.wait("vector", "ts")
                b.do("vector", lambda v, g=g: v.tensor_copy(
                    dst_sb[:, g * 512:(g + 1) * 512], ptp[:]), inc="vs")
                rec["ptp_vs"] = b.c["vs"]

        def ln_stats_apply(b, x_ap, width, out_ap, c0, eps_ap, qs_val):
            inv = 1.0 / width
            b.wait("vector", "qx", qs_val)
            b.do("vector", lambda v: v.scalar_tensor_tensor(
                scrV[:, 0:width], x_ap, inv, x_ap, ALU.mult, ALU.mult,
                accum_out=stat[:, c0:c0 + 1]), inc="vs")
            b.do("vector", lambda v: v.tensor_copy(
                stat[:, 30:31], cst[:, 0:1]))
            b.wait("scalar", "qx", qs_val)
            b.wait("scalar", "vs")
            b.do("scalar", lambda s: s.activation(
                scrS[:, 0:width], x_ap, AF.Identity, scale=inv,
                accum_out=stat[:, c0 + 1:c0 + 2]))
            b.do("scalar", lambda sc: sc.activation(
                stat[:, 29:30], cst[:, 0:1], AF.Identity), inc="ss")
            b.wait("vector", "ss")
            b.do("vector", lambda v: v.scalar_tensor_tensor(
                stat[:, c0 + 2:c0 + 3], stat[:, c0 + 1:c0 + 2],
                stat[:, c0 + 1:c0 + 2], stat[:, c0:c0 + 1],
                ALU.mult, ALU.subtract), inc="vs")
            b.wait("scalar", "vs")
            b.do("scalar", lambda s: s.activation(
                stat[:, c0 + 3:c0 + 4], stat[:, c0 + 2:c0 + 3], AF.Ln,
                scale=-1.0, bias=eps_ap))
            b.do("scalar", lambda s: s.activation(
                stat[:, c0 + 3:c0 + 4], stat[:, c0 + 3:c0 + 4], AF.Exp,
                scale=-0.5), inc="ss")
            b.wait("vector", "ss")
            b.do("vector", lambda v: v.tensor_scalar(
                out_ap, x_ap, stat[:, c0 + 1:c0 + 2], stat[:, c0 + 3:c0 + 4],
                ALU.subtract, ALU.mult), inc="vs")

        EM5 = float(np.exp(-5.0))

        def sfence(b, inc="ss"):
            b.do("scalar", lambda sc: sc.activation(
                stat[:, 29:30], cst[:, 0:1], AF.Identity), inc=inc)

        def vfence(b, inc="vs"):
            b.do("vector", lambda v: v.tensor_copy(
                stat[:, 30:31], cst[:, 0:1]), inc=inc)

        def dump(b, slot, src_ap, pdim, fdim):
            if not DBG_DUMP:
                return
            b.wait("gpsimd", "vs")
            b.wait("gpsimd", "ss")
            b.wait("gpsimd", "ts")
            b.do("gpsimd", lambda g: g.dma_start(
                dbg_h[slot, 0:pdim, 0:fdim], src_ap), inc="gs")

        def run(b: Pass):
            rec_init()
            qs_rec = {}
            for name, dst, src in [("i128", i128, i128_h), ("i128f", i128f, i128f_h),
                                   ("i16", i16, i16_h),
                                   ("mmask", mmask, mmask_h), ("nw", nw, nw_h),
                                   ("mur", mur, mur_h), ("muk", muk, muk_h)]:
                qs_rec[name] = b.do("sync", lambda q, d=dst, s=src:
                                    q.dma_start(d[:], s[:]), inc="qc")
            consts_qs = b.c["qc"]
            wsem = {"win": "w1", "wabc": "w2", "wout": "w3", "wr": "w4",
                    "wk": "w5", "wv": "w6"}
            for name, dst, src in [("win", win, win_h), ("wabc", wabc, wabc_h),
                                   ("wout", wout, wout_h), ("wr", wr, wr_h),
                                   ("wk", wk, wk_h), ("wv", wv, wv_h)]:
                b.do("gpsimd", lambda g, d=dst, s=src: g.dma_start(
                    d[:], s[:].rearrange("(k p) m -> p k m", p=128)),
                    inc=wsem[name])
            b.wait("vector", "w3", 16)
            b.wait("vector", "qc", consts_qs)
            for k in range(16):
                b.do("vector", lambda v, k=k: v.tensor_scalar(
                    wout[:, k, :], wout[:, k, :], nw[:, k:k + 1], None,
                    ALU.mult), inc="vs" if k == 15 else None)
            wout_ready_vs = b.c["vs"]

            b.do("vector", lambda v: v.memset(pmisc[:, 0:128], 0.0))
            b.do("vector", lambda v: v.memset(cst[:, 0:1], EPS_LN))
            b.do("vector", lambda v: v.memset(cst[:, 1:2], EPS_RMS))
            b.do("vector", lambda v: v.memset(state[:], 0.0))
            b.do("vector", lambda v: v.memset(ln2T[0][:], 0.0))
            b.do("vector", lambda v: v.memset(ln2T[1][:], 0.0), inc="vs")

            x_load_q = [0] * (2 * NT)
            x_store_q = [0] * (2 * NT)
            state_evac_vs = 0

            st_ln1T_ts = [0]
            lf_state = {"ln1_vs": [0, 0], "factors_vs": 0}

            def emit_lf(tF, tB):
                xbf = xt[tF % 3] if tF is not None else None
                c0 = 4 * (tF % 2) if tF is not None else 0
                if tF is not None and x_load_q[tF] == 0:
                    b.wait("sync", "vs")
                    x_load_q[tF] = b.do("sync", lambda q, tF=tF, xbf=xbf:
                        q.dma_start(xbf[:], x_h[tF * 128:(tF + 1) * 128, :]),
                        inc="qx")

                if tF is not None:
                    b.wait("vector", "qx", x_load_q[tF])
                    b.do("vector", lambda v: v.scalar_tensor_tensor(
                        scrV[:, 0:DM], xbf[:], 1.0 / DM, xbf[:], ALU.mult,
                        ALU.mult, accum_out=stat[:, c0:c0 + 1]), inc="vs")
                vfence(b)
                if tB is not None:
                    b.do("vector", lambda v: v.tensor_tensor_scan(
                        lp[:], e16[:], e16[:], 0.0, ALU.add, ALU.bypass))
                    vfence(b)
                    b.do("vector", lambda v: v.tensor_scalar(
                        sv16[:, 0:1], lp[:, 63:64], -1.0, None, ALU.mult))
                    b.do("vector", lambda v: v.tensor_scalar(
                        sv16[:, 1:2], lp[:, 127:128], -1.0, None, ALU.mult),
                        inc="vs")
                if tF is not None:
                    b.wait("scalar", "qx", x_load_q[tF])
                    b.wait("scalar", "vs")
                    b.do("scalar", lambda s: s.activation(
                        scrS[:, 0:DM], xbf[:], AF.Identity, scale=1.0 / DM,
                        accum_out=stat[:, c0 + 1:c0 + 2]))
                    sfence(b)
                    b.wait("vector", "ss")
                    b.do("vector", lambda v: v.scalar_tensor_tensor(
                        stat[:, c0 + 2:c0 + 3], stat[:, c0 + 1:c0 + 2], stat[:, c0 + 1:c0 + 2],
                        stat[:, c0:c0 + 1], ALU.mult, ALU.subtract), inc="vs")
                b.wait("scalar", "vs")
                if tF is not None:
                    b.do("scalar", lambda s: s.activation(
                        stat[:, c0 + 3:c0 + 4], stat[:, c0 + 2:c0 + 3], AF.Ln, scale=-1.0,
                        bias=cst[:, 0:1]))
                    b.do("scalar", lambda s: s.activation(
                        stat[:, c0 + 3:c0 + 4], stat[:, c0 + 3:c0 + 4], AF.Exp, scale=-0.5),
                        inc="ss" if tB is None else None)
                if tB is not None:
                    b.do("scalar", lambda s: s.activation(
                        exsc[:], lp[:], AF.Exp, scale=-1.0))
                    b.do("scalar", lambda s: s.activation(
                        exs2[:, 0:64], lp[:, 64:128], AF.Exp, scale=-1.0,
                        bias=lp[:, 63:64]))
                    b.do("scalar", lambda s: s.activation(
                        exs2[:, 64:128], lp[:, 0:64], AF.Exp))
                    b.do("scalar", lambda s: s.activation(
                        exs4[:, 0:64], lp[:, 64:128], AF.Exp,
                        bias=sv16[:, 0:1]))
                    b.do("scalar", lambda s: s.activation(
                        exs3[:], lp[:], AF.Exp, bias=sv16[:, 1:2]))
                    b.do("scalar", lambda s: s.activation(
                        sv16[:, 2:3], sv16[:, 0:1], AF.Exp))
                    b.do("scalar", lambda s: s.activation(
                        sv16[:, 3:4], sv16[:, 1:2], AF.Exp), inc="ss")
                b.wait("vector", "ss")
                if tF is not None:
                    b.wait("vector", "ts", st_ln1T_ts[0])
                    b.do("vector", lambda v, ln1=ln1b[tF % 2]: v.tensor_scalar(
                        ln1[:], xbf[:], stat[:, c0 + 1:c0 + 2], stat[:, c0 + 3:c0 + 4],
                        ALU.subtract, ALU.mult),
                        inc="vs" if tB is None else None)
                if tB is not None:
                    b.wait("vector", "qc", consts_qs)
                    B_ap = abc[:, 128:256]
                    C_ap = abc[:, 256:384]
                    b.do("vector", lambda v: v.tensor_tensor(
                        cint[:], exsc[:], C_ap, ALU.mult))
                    b.do("vector", lambda v: v.tensor_tensor(
                        cex2[:], exs2[:, 0:64], C_ap[:, 64:128], ALU.mult))
                    b.do("vector", lambda v: v.tensor_tensor(
                        bex1[:], exs2[:, 64:128], B_ap[:, 0:64], ALU.mult))
                    b.do("vector", lambda v: v.tensor_tensor(
                        bex2[:], exs4[:, 0:64], B_ap[:, 64:128], ALU.mult))
                    b.do("vector", lambda v: v.tensor_scalar(
                        bex1x[:], bex1[:], sv16[:, 2:3], None, ALU.mult))
                    b.do("vector", lambda v: v.tensor_tensor(
                        bsc[:], exs3[:], B_ap, ALU.mult))
                    b.do("vector", lambda v: v.tensor_scalar(
                        dgl[:], i16[:], sv16[:, 3:4], None, ALU.mult),
                        inc="vs")
                if False:
                    dump(b, 6, lp[:], 16, 128)
                    dump(b, 7, cint[:], 16, 128)
                    dump(b, 8, bex1[:], 16, 64)

                lf_state["factors_vs"] = b.c["vs"]
                if tF is not None:
                    lf_state["ln1_vs"][tF % 2] = b.c["vs"]

            emit_lf(0, None)
            emit_lf(1, None)
            for it in range(NT + 1):
                t = it
                tb = it - 1
                F = t < NT
                B = tb >= 0
                pf = t % 2
                pb = tb % 2
                xif = x_in[pf]
                xib = x_in[pb]
                xbf = xt[t % 3] if F else None
                xbb = xt[tb % 3] if B else None
                factors_vs = lf_state["factors_vs"]
                ln1_vs = lf_state["ln1_vs"][t % 2]

                if F:
                    b.wait("tensor", "qc", consts_qs)
                    transpose_group(b, ln1T, ln1b[t % 2], DM, ln1_vs)
                    st_ln1T_ts[0] = b.c["ts"]
                    rec["ln1T_vs"] = b.c["vs"]

                def xz_pass(jps):
                    b.wait("tensor", "vs",
                           max(rec["ln1T_vs"], rec["gate_vs"]))
                    b.wait("tensor", "ss", rec["gate_ss"])
                    b.wait("tensor", "w1", 16)
                    for k in range(8):
                        for half, pyb in ((0, py0), (1, py1)):
                            jp = jps[half]
                            b.do("tensor", lambda tt, k=k, jp=jp, pyb=pyb:
                                 tt.matmul(pyb[:, 0:512],
                                           ln1T[:, k * 128:(k + 1) * 128],
                                           win[:, k, jp * 512:(jp + 1) * 512],
                                           start=(k == 0), stop=(k == 7)))
                            b.do("tensor", lambda tt, k=k, jp=jp, pyb=pyb:
                                 tt.matmul(pyb[:, 512:1024],
                                           ln1T[:, k * 128:(k + 1) * 128],
                                           win[:, k, 2048 + jp * 512:
                                               2048 + (jp + 1) * 512],
                                           start=(k == 0), stop=(k == 7)),
                                 inc="ts" if (half == 1 and k == 7) else None)
                    b.wait("scalar", "ts")
                    for half, pyb, sa, sbuf_ in ((0, py0, s1b, s2b),
                                                 (1, py1, s1b2, s2b2)):
                        b.do("scalar", lambda s, pyb=pyb, sa=sa: s.activation(
                            sa[:], pyb[:, 0:512], AF.Sigmoid))
                        b.do("scalar", lambda s, pyb=pyb, sbuf_=sbuf_:
                             s.activation(sbuf_[:], pyb[:, 512:1024],
                                          AF.Sigmoid),
                             inc="ss" if half == 1 else None)
                    b.wait("vector", "ss")
                    for half, pyb, sa, sbuf_ in ((0, py0, s1b, s2b),
                                                 (1, py1, s1b2, s2b2)):
                        jp = jps[half]
                        b.do("vector", lambda v, sa=sa, sbuf_=sbuf_:
                             v.tensor_tensor(sa[:], sa[:], sbuf_[:], ALU.mult))
                        b.do("vector", lambda v, jp=jp, pyb=pyb, sa=sa:
                             v.tensor_tensor(xif[:, jp * 512:(jp + 1) * 512],
                                             pyb[:, 0:512], sa[:], ALU.mult),
                             inc="vs" if half == 1 else None)
                    rec["gate_vs"] = b.c["vs"]
                    rec["gate_ss"] = b.c["ss"]
                if F:
                    xz_pass((0, 1))

                if B:
                    b.wait("tensor", "vs", max(factors_vs, rec["ptp_vs"],
                                               rec["abc_tm_vs"], rec["mt_vs"]))
                    b.wait("tensor", "qc", consts_qs)
                    b.do("tensor", lambda tt: tt.matmul(
                        pmisc[0:64, 0:64], bex1[:], cint[:, 0:64],
                        start=True, stop=True))
                    b.do("tensor", lambda tt: tt.matmul(
                        pmisc[0:64, 64:128], bex1x[:], cex2[:],
                        start=True, stop=True))
                    b.do("tensor", lambda tt: tt.matmul(
                        pmisc[64:128, 64:128], bex2[:], cex2[:],
                        start=True, stop=True, tile_position=(0, 64)))
                    b.do("tensor", lambda tt: tt.transpose(
                        ptp[:, 0:16], bsc[:], i16[:]), inc="ts")
                    b.wait("vector", "ts")
                    b.wait("vector", "qc", consts_qs)
                    b.do("vector", lambda v: v.tensor_tensor(
                        mt[:], pmisc[:, 0:128], mmask[:], ALU.mult))
                    b.do("vector", lambda v: v.tensor_copy(
                        bscT[:], ptp[:, 0:16]), inc="vs")
                    mt_vs = b.c["vs"]
                    rec["mt_vs"] = mt_vs
                    rec["ptp_vs"] = mt_vs
                if B and tb == 0:
                    dump(b, 9, mt[:], 128, 128)


                if B:
                    b.wait("tensor", "vs", max(mt_vs, state_evac_vs,
                                               rec["postln_vs"], rec["bc_vs"]))
                    b.wait("tensor", "ss", max(rec["postln_ss"], rec["ez_ss"]))
                    regs = [(po, 0), (po, 512), (pmisc, 0)]
                    ssq_cols = [8, 9, 10, 11]
                    out_evac_vs = {}
                    for dc in range(3):
                        reg, roff = regs[dc]
                        b.do("tensor", lambda tt, reg=reg, roff=roff, dc=dc:
                             tt.matmul(reg[:, roff:roff + 512], cint[:],
                                       state[:, dc * 512:(dc + 1) * 512],
                                       start=True, stop=False))
                        b.do("tensor", lambda tt, reg=reg, roff=roff, dc=dc:
                             tt.matmul(reg[:, roff:roff + 512], mt[:],
                                       xib[:, dc * 512:(dc + 1) * 512],
                                       start=False, stop=True), inc="ts")
                        b.wait("vector", "ts")
                        out_evac_vs[dc] = b.do("vector",
                            lambda v, reg=reg, roff=roff, dc=dc: v.tensor_copy(
                                out_n[:, dc * 512:(dc + 1) * 512],
                                reg[:, roff:roff + 512]), inc="vs")
                        b.wait("scalar", "vs", out_evac_vs[dc])
                        b.do("scalar", lambda s, dc=dc, c=ssq_cols[dc]:
                             s.activation(scrS[:, 0:512],
                                          out_n[:, dc * 512:(dc + 1) * 512],
                                          AF.Square, scale=SQDI,
                                          accum_out=stat[:, c:c + 1]),
                             inc="ss")

                if F:
                    xz_pass((2, 3))
                    gating_vs = b.c["vs"]

                if B:
                    b.wait("tensor", "vs", out_evac_vs[0])
                    b.do("tensor", lambda tt: tt.matmul(
                        po[:, 0:512], cint[:], state[:, 1536:2048],
                        start=True, stop=False))
                    b.do("tensor", lambda tt: tt.matmul(
                        po[:, 0:512], mt[:], xib[:, 1536:2048],
                        start=False, stop=True), inc="ts")
                    b.wait("vector", "ts")
                    out_evac_vs[3] = b.do("vector", lambda v: v.tensor_copy(
                        out_n[:, 1536:2048], po[:, 0:512]), inc="vs")
                    b.wait("scalar", "vs", out_evac_vs[3])
                    b.do("scalar", lambda s: s.activation(
                        scrS[:, 0:512], out_n[:, 1536:2048], AF.Square,
                        scale=SQDI, accum_out=stat[:, 11:12]))
                    sfence(b)

                if B:
                    b.wait("tensor", "ss", rec["ez_ss"])
                    sregs = [(pmisc, 0, out_evac_vs[2]),
                             (po, 512, out_evac_vs[1]),
                             (po, 0, out_evac_vs[3])]
                    st_evac = {}
                    for dc in range(3):
                        reg, roff, wvv = sregs[dc]
                        b.wait("tensor", "vs", wvv)
                        b.do("tensor", lambda tt, reg=reg, roff=roff, dc=dc:
                             tt.matmul(reg[0:16, roff:roff + 512], dgl[:],
                                       state[:, dc * 512:(dc + 1) * 512],
                                       start=True, stop=False))
                        b.do("tensor", lambda tt, reg=reg, roff=roff, dc=dc:
                             tt.matmul(reg[0:16, roff:roff + 512], bscT[:],
                                       xib[:, dc * 512:(dc + 1) * 512],
                                       start=False, stop=True), inc="ts")
                        b.wait("vector", "ts")
                        st_evac[dc] = b.do("vector",
                            lambda v, reg=reg, roff=roff, dc=dc: v.tensor_copy(
                                state[:, dc * 512:(dc + 1) * 512],
                                reg[0:16, roff:roff + 512]), inc="vs")

                if F:
                    transpose_group(b, x_inT, xif, DI, rec["gate_vs"])
                    x_inT_vs = b.c["vs"]
                if F and t == 0:
                    dump(b, 1, xif[:], 128, DI)
                    dump(b, 2, x_inT[:], 128, DI)

                if B:
                    b.wait("tensor", "vs", st_evac[0])
                    b.do("tensor", lambda tt: tt.matmul(
                        pmisc[0:16, 0:512], dgl[:], state[:, 1536:2048],
                        start=True, stop=False))
                    b.do("tensor", lambda tt: tt.matmul(
                        pmisc[0:16, 0:512], bscT[:], xib[:, 1536:2048],
                        start=False, stop=True), inc="ts")
                    b.wait("vector", "ts")
                    st_evac[3] = b.do("vector", lambda v: v.tensor_copy(
                        state[:, 1536:2048], pmisc[0:16, 0:512]), inc="vs")
                    state_evac_vs = st_evac[3]

                if B:
                    b.wait("vector", "ss")
                    b.do("vector", lambda v: v.tensor_reduce(
                        stat[:, 24:25], stat[:, 8:12], mybir.AxisListType.X,
                        ALU.add), inc="vs")
                    b.wait("scalar", "vs")
                    b.do("scalar", lambda s: s.activation(
                        stat[:, 25:26], stat[:, 24:25], AF.Ln,
                        scale=1.0, bias=cst[:, 1:2]))
                    b.do("scalar", lambda s: s.activation(
                        stat[:, 25:26], stat[:, 25:26], AF.Exp, scale=-0.5),
                        inc="ss")
                    if DBG_DUMP and tb == 0:
                        b.wait("gpsimd", "vs")
                        b.do("gpsimd", lambda g: g.dma_start(
                            dbg_h[2, :, 0:DI], out_n[:]), inc="gs")
                        b.wait("vector", "gs")
                    b.wait("vector", "ss")
                    b.do("vector", lambda v: v.tensor_scalar(
                        out_n[:], out_n[:], stat[:, 25:26], None, ALU.mult),
                        inc="vs")
                    rec["rms_vs"] = b.c["vs"]
                if B and tb == 0:
                    dump(b, 10, out_n[:], 128, DI)
                    dump(b, 11, stat[:], 128, 32)

                if F:
                    b.wait("tensor", "vs", x_inT_vs)
                    if B:
                        b.wait("tensor", "vs", st_evac[3])
                    b.wait("tensor", "w2", 16)
                    for k in range(16):
                        b.do("tensor", lambda tt, k=k: tt.matmul(
                            pmisc[:, 0:48], x_inT[:, k * 128:(k + 1) * 128],
                            wabc[:, k, :], start=(k == 0), stop=(k == 15)),
                            inc="ts" if k == 15 else None)
                    b.wait("vector", "ts")
                    b.do("vector", lambda v: v.tensor_copy(
                        abc_tm[:], pmisc[:, 0:48]), inc="vs")
                    abc_tm_vs = b.c["vs"]
                    rec["abc_tm_vs"] = abc_tm_vs
                if F and t == 0:
                    dump(b, 3, abc_tm[:], 128, 48)

                if B:
                    transpose_group(b, out_nT, out_n, DI, rec["rms_vs"])
                    out_nT_vs = b.c["vs"]

                if F:
                    b.wait("tensor", "vs", max(abc_tm_vs, rec["bc_vs"]))
                    b.wait("tensor", "ss", rec["ez_ss"])
                    b.wait("tensor", "qc", consts_qs)
                    for j in range(3):
                        b.do("tensor", lambda tt, j=j: tt.transpose(
                            pmisc[0:16, 128 + j * 128:256 + j * 128],
                            abc_tm[:, j * 16:(j + 1) * 16], i128f[:]),
                            inc="ts" if j == 2 else None)
                    b.wait("scalar", "ts")
                    b.do("scalar", lambda s: s.activation(
                        e16[:], pmisc[0:16, 128:256], AF.Exp), inc="ss")
                    b.wait("vector", "ts")
                    b.wait("vector", "ss")
                    b.do("vector", lambda v: v.tensor_copy(
                        abc[:, 128:256], pmisc[0:16, 256:384]))
                    b.do("vector", lambda v: v.tensor_copy(
                        abc[:, 256:384], pmisc[0:16, 384:512]))
                    b.do("vector", lambda v: v.tensor_scalar(
                        e16[:], e16[:], EM5, 1.0, ALU.max, ALU.min),
                        inc="vs")
                    rec["bc_vs"] = b.c["vs"]
                    rec["ez_ss"] = b.c["ss"]
                if F and t == 0:
                    dump(b, 4, e16[:], 16, 128)
                    dump(b, 5, abc[:, 128:384], 16, 256)

                if B:
                    b.wait("tensor", "vs", out_nT_vs)
                    b.wait("tensor", "vs", wout_ready_vs)
                    for k in range(16):
                        for mc in range(2):
                            b.do("tensor", lambda tt, mc=mc, k=k: tt.matmul(
                                po[:, mc * 512:(mc + 1) * 512],
                                out_nT[:, k * 128:(k + 1) * 128],
                                wout[:, k, mc * 512:(mc + 1) * 512],
                                start=(k == 0), stop=(k == 15)),
                                inc="ts" if (mc == 1 and k == 15) else None)
                    b.wait("scalar", "ts")
                    b.do("scalar", lambda s: s.activation(
                        scrS[:, 0:1024], po[:], AF.Square, scale=SQDM,
                        accum_out=stat[:, 12:13]))
                    b.do("scalar", lambda s: s.activation(
                        scrS[:, 0:1024], po[:], AF.Identity, scale=1.0 / DM,
                        accum_out=stat[:, 13:14]))
                    sfence(b)
                    b.wait("vector", "ss")
                    b.do("vector", lambda v: v.scalar_tensor_tensor(
                        stat[:, 14:15], stat[:, 13:14], stat[:, 13:14],
                        stat[:, 12:13], ALU.mult, ALU.subtract), inc="vs")
                    b.wait("scalar", "vs")
                    b.do("scalar", lambda s: s.activation(
                        stat[:, 15:16], stat[:, 14:15], AF.Ln, scale=-1.0,
                        bias=cst[:, 0:1]))
                    b.do("scalar", lambda s: s.activation(
                        stat[:, 15:16], stat[:, 15:16], AF.Exp, scale=-0.5),
                        inc="ss")
                    rec["postln_ss"] = b.c["ss"]
                    b.wait("vector", "ss")
                    b.wait("vector", "qy", x_store_q[tb - 1] if tb >= 1 else 0)
                    b.do("vector", lambda v: v.tensor_scalar(
                        tscr[:], po[:], stat[:, 13:14], stat[:, 15:16],
                        ALU.subtract, ALU.mult))
                    b.do("vector", lambda v, xbb=xbb: v.tensor_tensor(
                        tscr[:], tscr[:], xbb[:], ALU.add), inc="vs")
                    rec["postln_vs"] = b.c["vs"]
                    b.wait("sync", "vs")
                    x_store_q[tb] = b.do("sync", lambda q, tb=tb:
                        q.dma_start(y_h[tb * 128:(tb + 1) * 128, :], tscr[:]),
                        inc="qy")

                emit_lf(t + 2 if t + 2 < NT else None,
                        t if t < NT else None)

            phaseA_done_q = b.c["qy"]

            for it in range(NT + 1):
                u = it
                ub = it - 1
                F = u < NT
                B = ub >= 0
                xbf = xt[u % 3] if F else None
                xbb = xt[ub % 3] if B else None

                if F:
                    b.wait("sync", "qy", phaseA_done_q)
                    if u >= 2:
                        b.wait("sync", "qy", x_store_q[NT + u - 2])
                    x_load_q[NT + u] = b.do("sync", lambda q, u=u, xbf=xbf:
                        q.dma_start(xbf[:], y_h[u * 128:(u + 1) * 128, :]),
                        inc="qx")

                if B:
                    b.wait("tensor", "vs", max(rec["mix_vs"], rec["kact_vs"]))
                    b.wait("tensor", "ss", rec["rb_ss"])
                    b.wait("tensor", "w4", 16)
                    b.wait("tensor", "w5", 16)
                    for k in range(8):
                        for mc in range(2):
                            b.do("tensor", lambda tt, mc=mc, k=k: tt.matmul(
                                py0[:, mc * 512:(mc + 1) * 512],
                                xrT[:, k * 128:(k + 1) * 128],
                                wr[:, k, mc * 512:(mc + 1) * 512],
                                start=(k == 0), stop=(k == 7)))
                    for k in range(8):
                        for mc in range(2):
                            b.do("tensor", lambda tt, mc=mc, k=k: tt.matmul(
                                py1[:, mc * 512:(mc + 1) * 512],
                                xkT[:, k * 128:(k + 1) * 128],
                                wk[:, k, mc * 512:(mc + 1) * 512],
                                start=(k == 0), stop=(k == 7)),
                                inc="ts" if (mc == 1 and k == 7) else None)
                    rk_ts = b.c["ts"]

                ln2 = ln1b[u % 2] if F else None
                if F:
                    ln_stats_apply(b, xbf[:], DM, ln2[:], 0, cst[:, 0:1],
                                   x_load_q[NT + u])
                    rec["ln2_vs"] = b.c["vs"]

                if B:
                    b.wait("scalar", "ts", rk_ts)
                    b.do("scalar", lambda s: s.activation(
                        r_b, py0[:], AF.Sigmoid), inc="ss")
                    rec["rb_ss"] = b.c["ss"]
                    b.wait("vector", "ts", rk_ts)
                    b.do("vector", lambda v: v.tensor_scalar(
                        tscr[:], py1[:], 0.0, None, ALU.max))
                    b.do("vector", lambda v: v.tensor_tensor(
                        kact, tscr[:], tscr[:], ALU.mult), inc="vs")
                    kact_vs = b.c["vs"]
                    rec["kact_vs"] = kact_vs

                if F:
                    par = u % 2
                    lnb = ln2T[par]
                    lnprev = ln2T[1 - par]
                    for g in range(2):
                        b.wait("tensor", "vs", max(rec["ln2_vs"], rec["ptp_vs"]))
                        b.wait("tensor", "qc", consts_qs)
                        for i in range(4):
                            k = g * 4 + i
                            b.do("tensor", lambda tt, k=k, i=i: tt.transpose(
                                ptp[:, i * 128:(i + 1) * 128],
                                ln2[:, k * 128:(k + 1) * 128], i128[:]),
                                inc="ts" if i == 3 else None)
                        b.wait("vector", "ts")
                        for i in range(4):
                            k = g * 4 + i
                            b.do("vector", lambda v, k=k, i=i: v.tensor_copy(
                                lnb[:, k, 1:129],
                                ptp[:, i * 128:(i + 1) * 128]),
                                inc="vs" if i == 3 else None)
                        rec["ptp_vs"] = b.c["vs"]
                    if u > 0:
                        for k in range(8):
                            b.do("vector", lambda v, k=k: v.tensor_copy(
                                lnb[:, k, 0:1], lnprev[:, k, 128:129]),
                                inc="vs" if k == 7 else None)
                    b.wait("vector", "qc", consts_qs)
                    for k in range(8):
                        b.do("vector", lambda v, k=k: v.tensor_tensor(
                            dmix[:], lnb[:, k, 1:129], lnb[:, k, 0:128],
                            ALU.subtract))
                        b.do("vector", lambda v, k=k: v.scalar_tensor_tensor(
                            xrT[:, k * 128:(k + 1) * 128], dmix[:],
                            mur[:, k:k + 1], lnb[:, k, 0:128],
                            ALU.mult, ALU.add))
                        b.do("vector", lambda v, k=k: v.scalar_tensor_tensor(
                            xkT[:, k * 128:(k + 1) * 128], dmix[:],
                            muk[:, k:k + 1], lnb[:, k, 0:128],
                            ALU.mult, ALU.add), inc="vs" if k == 7 else None)
                    rec["mix_vs"] = b.c["vs"]

                if B:
                    b.wait("tensor", "vs", kact_vs)
                    transpose_group(b, kactT, kact, DM, rec["kact_vs"])
                    kactT_vs = b.c["vs"]
                    b.wait("tensor", "vs", max(kactT_vs, rec["fout_vs"]))
                    b.wait("tensor", "ss", rec["rb_ss"])
                    b.wait("tensor", "w6", 16)
                    for k in range(8):
                        for mc in range(2):
                            b.do("tensor", lambda tt, mc=mc, k=k: tt.matmul(
                                po[:, mc * 512:(mc + 1) * 512],
                                kactT[:, k * 128:(k + 1) * 128],
                                wv[:, k, mc * 512:(mc + 1) * 512],
                                start=(k == 0), stop=(k == 7)),
                                inc="ts" if (mc == 1 and k == 7) else None)
                    b.wait("vector", "ts")
                    b.wait("vector", "qy",
                           x_store_q[NT + ub - 1] if ub >= 1 else phaseA_done_q)
                    b.do("vector", lambda v: v.tensor_tensor(
                        fout[:], po[:], r_b, ALU.mult), inc="vs")
                    rec["fout_vs"] = b.c["vs"]
                    if DBG_DUMP and ub == 0:
                        dump(b, 2, xbb[:], 128, DM)
                        dump(b, 0, fout[:], 128, DM)
                        dump(b, 1, r_b, 128, DM)
                        dump(b, 4, kact, 128, DM)
                        dump(b, 5, xrT[:, 0:DM], 128, DM)
                        dump(b, 6, xkT[:, 0:DM], 128, DM)
                        dump(b, 7, ln2[:], 128, DM)
                    b.do("vector", lambda v: v.scalar_tensor_tensor(
                        scrV[:, 0:1024], fout[:], 1.0 / DM, fout[:],
                        ALU.mult, ALU.mult, accum_out=stat[:, 16:17]))
                    vfence(b)
                    b.wait("scalar", "vs")
                    b.do("scalar", lambda s: s.activation(
                        scrS[:, 0:1024], fout[:], AF.Identity,
                        scale=1.0 / DM, accum_out=stat[:, 17:18]))
                    sfence(b)
                    b.wait("vector", "ss")
                    b.do("vector", lambda v: v.scalar_tensor_tensor(
                        stat[:, 18:19], stat[:, 17:18], stat[:, 17:18],
                        stat[:, 16:17], ALU.mult, ALU.subtract), inc="vs")
                    b.wait("scalar", "vs")
                    b.do("scalar", lambda s: s.activation(
                        stat[:, 19:20], stat[:, 18:19], AF.Ln, scale=-1.0,
                        bias=cst[:, 0:1]))
                    b.do("scalar", lambda s: s.activation(
                        stat[:, 19:20], stat[:, 19:20], AF.Exp, scale=-0.5),
                        inc="ss")
                    b.wait("vector", "ss")
                    b.do("vector", lambda v: v.tensor_scalar(
                        fout[:], fout[:], stat[:, 17:18], stat[:, 19:20],
                        ALU.subtract, ALU.mult))
                    b.do("vector", lambda v, xbb=xbb: v.tensor_tensor(
                        xbb[:], fout[:], xbb[:], ALU.add), inc="vs")
                    if DBG_DUMP and ub == 0:
                        dump(b, 8, stat[:], 128, 32)
                        dump(b, 3, xbb[:], 128, DM)
                    b.wait("sync", "vs")
                    x_store_q[NT + ub] = b.do("sync", lambda q, ub=ub, xbb=xbb:
                        q.dma_start(y_h[ub * 128:(ub + 1) * 128, :], xbb[:]),
                        inc="qy")

            b.wait("sync", "qx", b.c["qx"])
            b.wait("sync", "qy", b.c["qy"])
            for _wn in ("w1", "w2", "w3", "w4", "w5", "w6"):
                b.wait("gpsimd", _wn, 16)

        @blk.sync
        def _(sync):
            run(Pass("sync", sync, sems))

        @blk.gpsimd
        def _(g):
            run(Pass("gpsimd", g, sems))

        @blk.tensor
        def _(tt):
            run(Pass("tensor", tt, sems))

        @blk.vector
        def _(v):
            run(Pass("vector", v, sems))

        @blk.scalar
        def _(s):
            run(Pass("scalar", s, sems))

    return nc


_NC_CACHE = None


def kernel(L0_attn_x, L0_attn_W_in, L0_attn_W_A, L0_attn_W_B, L0_attn_W_C,
           L0_attn_W_out, L0_attn_norm_w, L0_ffn_W_r, L0_ffn_W_k, L0_ffn_W_v,
           L0_ffn_mu_r, L0_ffn_mu_k):
    global _NC_CACHE
    x = np.ascontiguousarray(np.asarray(L0_attn_x, dtype=np.float32))
    w_in = np.ascontiguousarray(np.asarray(L0_attn_W_in, dtype=np.float32))
    w_abc = np.ascontiguousarray(np.concatenate(
        [np.asarray(L0_attn_W_A, dtype=np.float32),
         np.asarray(L0_attn_W_B, dtype=np.float32),
         np.asarray(L0_attn_W_C, dtype=np.float32)], axis=1))
    w_out = np.ascontiguousarray(np.asarray(L0_attn_W_out, dtype=np.float32))
    nw = np.ascontiguousarray(
        np.asarray(L0_attn_norm_w, dtype=np.float32).reshape(16, 128).T)
    w_rt = np.ascontiguousarray(np.asarray(L0_ffn_W_r, dtype=np.float32).T)
    w_kt = np.ascontiguousarray(np.asarray(L0_ffn_W_k, dtype=np.float32).T)
    w_vt = np.ascontiguousarray(np.asarray(L0_ffn_W_v, dtype=np.float32).T)
    mur = np.ascontiguousarray(
        np.asarray(L0_ffn_mu_r, dtype=np.float32).reshape(8, 128).T)
    muk = np.ascontiguousarray(
        np.asarray(L0_ffn_mu_k, dtype=np.float32).reshape(8, 128).T)
    i128 = np.eye(128, dtype=np.float32).astype(bfnp)
    i128f = np.eye(128, dtype=np.float32)
    i16 = np.eye(16, dtype=np.float32).astype(bfnp)
    H = 64
    mm = np.zeros((128, 128), np.float32)
    mm[:H, :H] = np.triu(np.ones((H, H)))
    mm[H:, H:] = np.triu(np.ones((H, H)))
    mm[:H, H:] = 1.0
    mmask = mm.astype(bfnp)

    if _NC_CACHE is None:
        _NC_CACHE = build_nc()
    nc = _NC_CACHE

    shared = dict(w_in=w_in, w_abc=w_abc, w_out=w_out, nw=nw, w_rt=w_rt,
                  w_kt=w_kt, w_vt=w_vt, mur=mur, muk=muk, i128=i128, i128f=i128f,
                  i16=i16, mmask=mmask)
    in_maps = [dict(x=np.ascontiguousarray(x[c]), **shared) for c in range(8)]
    res = run_bass_kernel_spmd(nc, in_maps, core_ids=list(range(8)))
    out = np.stack([np.asarray(r["y"]) for r in res.results], axis=0)
    return out.astype(np.float32)


# version 21 (final)
# speedup vs baseline: 1.1701x; 1.0727x over previous
import numpy as np
import ml_dtypes
import concourse.bass as bass
import concourse.mybir as mybir
from concourse.bass_utils import run_bass_kernel_spmd

F32 = mybir.dt.float32
F8 = mybir.dt.float8e4
BF16 = mybir.dt.bfloat16
AF = mybir.ActivationFunctionType
ALU = mybir.AluOpType
bfnp = ml_dtypes.bfloat16

T_TOK = 2048
DM = 1024
DI = 2048
N16 = 16
NT = 16
EPS_LN = 1e-5
EPS_RMS = float(np.finfo(np.float32).eps)
SQDI = float(np.sqrt(1.0 / 2048.0))
SQDM = float(np.sqrt(1.0 / 1024.0))

ENGS = ("sync", "gpsimd", "tensor", "vector", "scalar")
DBG_DUMP = False
SEMS = ("qs", "gs", "ts", "vs", "ss", "qc", "qx", "qy",
        "w1", "w2", "w3", "w4", "w5", "w6", "w7")


class Pass:

    def __init__(self, name, eng, sems):
        self.name = name
        self.eng = eng
        self.sems = sems
        self.c = {s: 0 for s in SEMS}

    def on(self, n):
        return self.name == n

    def do(self, eng_name, fn, inc=None):
        if self.name == eng_name:
            ins = fn(self.eng)
            if inc is not None:
                ins.then_inc(self.sems[inc], 1 if inc in ("ts", "vs", "ss") else 16)
        if inc is not None:
            self.c[inc] += 1 if inc in ("ts", "vs", "ss") else 16
            return self.c[inc]
        return None

    def wait(self, eng_name, sem, val=None):
        if val is None:
            val = self.c[sem]
        if val <= 0:
            return
        if self.name == eng_name:
            self.eng.wait_ge(self.sems[sem], val)


def build_nc():
    nc = bass.Bass("TRN2", target_bir_lowering=False)

    x_h = nc.dram_tensor("x", [T_TOK, DM], F32, kind="ExternalInput")
    win_h = nc.dram_tensor("w_in", [DM, 2 * DI], F32, kind="ExternalInput")
    wabc_h = nc.dram_tensor("w_abc", [DI, 48], F32, kind="ExternalInput")
    wout_h = nc.dram_tensor("w_out", [DI, DM], F32, kind="ExternalInput")
    nw_h = nc.dram_tensor("nw", [128, 16], F32, kind="ExternalInput")
    wr_h = nc.dram_tensor("w_rt", [DM, DM], F32, kind="ExternalInput")
    wk_h = nc.dram_tensor("w_kt", [DM, DM], F32, kind="ExternalInput")
    wv_h = nc.dram_tensor("w_vt", [DM, DM], F32, kind="ExternalInput")
    mur_h = nc.dram_tensor("mur", [128, 8], F32, kind="ExternalInput")
    muk_h = nc.dram_tensor("muk", [128, 8], F32, kind="ExternalInput")
    i128_h = nc.dram_tensor("i128", [128, 128], BF16, kind="ExternalInput")
    i128f_h = nc.dram_tensor("i128f", [128, 128], F32, kind="ExternalInput")
    i16_h = nc.dram_tensor("i16", [16, 16], BF16, kind="ExternalInput")
    mmask_h = nc.dram_tensor("mmask", [128, 128], BF16, kind="ExternalInput")
    y_h = nc.dram_tensor("y", [T_TOK, DM], F32, kind="ExternalOutput")
    if DBG_DUMP:
        dbg_h = nc.dram_tensor("dbg", [12, 128, 2048], F32, kind="ExternalOutput")

    from contextlib import ExitStack
    with ExitStack() as st:
        sb = lambda name, shape, dt: st.enter_context(nc.sbuf_tensor(name, shape, dt))
        psum = lambda name, shape, dt: st.enter_context(nc.psum_tensor(name, shape, dt))

        win = sb("win", [128, 8, 4096], BF16)
        wabc = sb("wabc", [128, 16, 48], BF16)
        wout = sb("wout", [128, 16, 1024], BF16)
        wr = sb("wr", [128, 8, 1024], BF16)
        wk = sb("wk", [128, 8, 1024], BF16)
        wv = sb("wv", [128, 8, 1024], BF16)
        nw = sb("nw_s", [128, 16], F32)
        i128 = sb("i128_s", [128, 128], BF16)
        i128f = sb("i128f_s", [128, 128], F32)
        i16 = sb("i16_s", [16, 16], BF16)
        mmask = sb("mmask_s", [128, 128], BF16)

        xt = [sb(f"xt{i}", [128, DM], F32) for i in range(3)]
        ln1b = [sb(f"ln1_{i}", [128, DM], BF16) for i in range(2)]
        ln1T = sb("ln1T", [128, DM], BF16)
        s1b = sb("s1b", [128, 512], BF16)
        s2b = sb("s2b", [128, 512], BF16)
        s1b2 = sb("s1b2", [128, 512], BF16)
        s2b2 = sb("s2b2", [128, 512], BF16)
        x_in = [sb(f"x_in{i}", [128, DI], BF16) for i in range(2)]
        x_inT = sb("x_inT", [128, DI], BF16)
        abc = sb("abc", [16, 3 * 128], F32)
        abc_tm = sb("abc_tm", [128, 48], F32)
        e16 = sb("e16", [16, 128], F32)
        lp = sb("lp", [16, 128], F32)
        exsc = sb("exsc", [16, 128], F32)
        exs2 = sb("exs2", [16, 128], F32)
        exs3 = sb("exs3", [16, 128], F32)
        exs4 = sb("exs4", [16, 64], F32)
        cint = sb("cint", [16, 128], BF16)
        cex2 = sb("cex2", [16, 64], BF16)
        bex1 = sb("bex1", [16, 64], BF16)
        bex1x = sb("bex1x", [16, 64], BF16)
        bex2 = sb("bex2", [16, 64], BF16)
        bsc = sb("bsc", [16, 128], BF16)
        bscT = sb("bscT", [128, 16], BF16)
        sv16 = sb("sv16", [16, 8], F32)
        dgl = sb("dgl", [16, 16], BF16)
        mt = sb("mt", [128, 128], BF16)
        state = sb("state", [16, DI], BF16)
        out_n = sb("out_n", [128, DI], BF16)
        out_nT = sb("out_nT", [128, DI], BF16)
        tscr = sb("tscr", [128, DM], F32)
        scrS = sb("scrS", [128, DM], F8)
        scrV = scrS
        stat = sb("stat", [128, 32], F32)
        cst = sb("cst", [128, 2], F32)
        ln2T = [sb(f"ln2T{i}", [128, 8, 130], BF16) for i in range(2)]
        mur = sb("mur_s", [128, 8], F32)
        muk = sb("muk_s", [128, 8], F32)
        dmix = mt
        xrT = ln1T
        xkT = x_inT
        r_b = x_in[0][:, 0:DM]
        kact = x_in[0][:, DM:2 * DM]
        kactT = out_nT
        fout = tscr

        py0 = psum("py0", [128, 1024], F32)
        py1 = psum("py1", [128, 1024], F32)
        po = psum("po", [128, 1024], F32)
        ptp = psum("ptp", [128, 512], BF16)
        pmisc = psum("pm", [128, 512], F32)

        sems = {s: st.enter_context(nc.semaphore(s)) for s in SEMS}
        blk = st.enter_context(nc.Block())


        rec = {}

        def rec_init():
            rec.clear()
            rec.update({"ptp_vs": 0, "ln1T_vs": 0, "gate_vs": 0, "gate_ss": 0,
                        "mt_vs": 0, "abc_tm_vs": 0, "ez_ss": 0, "bc_vs": 0,
                        "postln_vs": 0, "postln_ss": 0, "rms_vs": 0,
                        "mix_vs": 0, "rb_ss": 0, "kact_vs": 0, "fout_vs": 0,
                        "ln2_vs": 0})

        def transpose_group(b, dst_sb, src_sb, cols, src_vs):
            ngrp = cols // 512
            for g in range(ngrp):
                b.wait("tensor", "vs", max(src_vs, rec["ptp_vs"]))
                for i in range(4):
                    k = g * 4 + i
                    b.do("tensor", lambda t, k=k, i=i: t.transpose(
                        ptp[:, i * 128:(i + 1) * 128],
                        src_sb[:, k * 128:(k + 1) * 128], i128[:]),
                        inc="ts" if i == 3 else None)
                b.wait("vector", "ts")
                b.do("vector", lambda v, g=g: v.tensor_copy(
                    dst_sb[:, g * 512:(g + 1) * 512], ptp[:]), inc="vs")
                rec["ptp_vs"] = b.c["vs"]

        def ln_stats_apply(b, x_ap, width, out_ap, c0, eps_ap, qs_val):
            inv = 1.0 / width
            b.wait("vector", "qx", qs_val)
            b.do("vector", lambda v: v.scalar_tensor_tensor(
                scrV[:, 0:width], x_ap, inv, x_ap, ALU.mult, ALU.mult,
                accum_out=stat[:, c0:c0 + 1]), inc="vs")
            b.do("vector", lambda v: v.tensor_copy(
                stat[:, 30:31], cst[:, 0:1]))
            b.wait("scalar", "qx", qs_val)
            b.wait("scalar", "vs")
            b.do("scalar", lambda s: s.activation(
                scrS[:, 0:width], x_ap, AF.Identity, scale=inv,
                accum_out=stat[:, c0 + 1:c0 + 2]))
            b.do("scalar", lambda sc: sc.activation(
                stat[:, 29:30], cst[:, 0:1], AF.Identity), inc="ss")
            b.wait("vector", "ss")
            b.do("vector", lambda v: v.scalar_tensor_tensor(
                stat[:, c0 + 2:c0 + 3], stat[:, c0 + 1:c0 + 2],
                stat[:, c0 + 1:c0 + 2], stat[:, c0:c0 + 1],
                ALU.mult, ALU.subtract), inc="vs")
            b.wait("scalar", "vs")
            b.do("scalar", lambda s: s.activation(
                stat[:, c0 + 3:c0 + 4], stat[:, c0 + 2:c0 + 3], AF.Ln,
                scale=-1.0, bias=eps_ap))
            b.do("scalar", lambda s: s.activation(
                stat[:, c0 + 3:c0 + 4], stat[:, c0 + 3:c0 + 4], AF.Exp,
                scale=-0.5), inc="ss")
            b.wait("vector", "ss")
            b.do("vector", lambda v: v.tensor_scalar(
                out_ap, x_ap, stat[:, c0 + 1:c0 + 2], stat[:, c0 + 3:c0 + 4],
                ALU.subtract, ALU.mult), inc="vs")

        EM5 = float(np.exp(-5.0))

        def sfence(b, inc="ss"):
            b.do("scalar", lambda sc: sc.activation(
                stat[:, 29:30], cst[:, 0:1], AF.Identity), inc=inc)

        def vfence(b, inc="vs"):
            b.do("vector", lambda v: v.tensor_copy(
                stat[:, 30:31], cst[:, 0:1]), inc=inc)

        def dump(b, slot, src_ap, pdim, fdim):
            if not DBG_DUMP:
                return
            b.wait("gpsimd", "vs")
            b.wait("gpsimd", "ss")
            b.wait("gpsimd", "ts")
            b.do("gpsimd", lambda g: g.dma_start(
                dbg_h[slot, 0:pdim, 0:fdim], src_ap), inc="gs")

        def run(b: Pass):
            rec_init()
            qs_rec = {}
            for name, dst, src in [("i128", i128, i128_h), ("i128f", i128f, i128f_h),
                                   ("i16", i16, i16_h),
                                   ("mmask", mmask, mmask_h), ("nw", nw, nw_h),
                                   ("mur", mur, mur_h), ("muk", muk, muk_h)]:
                qs_rec[name] = b.do("sync", lambda q, d=dst, s=src:
                                    q.dma_start(d[:], s[:]), inc="qc")
            consts_qs = b.c["qc"]
            wsem = {"win": "w1", "wabc": "w2", "wout": "w3", "wr": "w4",
                    "wk": "w5", "wv": "w6"}
            for name, dst, src in [("win", win, win_h), ("wabc", wabc, wabc_h),
                                   ("wout", wout, wout_h), ("wr", wr, wr_h),
                                   ("wk", wk, wk_h), ("wv", wv, wv_h)]:
                b.do("gpsimd", lambda g, d=dst, s=src: g.dma_start(
                    d[:], s[:].rearrange("(k p) m -> p k m", p=128)),
                    inc=wsem[name])
            b.wait("vector", "w3", 16)
            b.wait("vector", "qc", consts_qs)
            for k in range(16):
                b.do("vector", lambda v, k=k: v.tensor_scalar(
                    wout[:, k, :], wout[:, k, :], nw[:, k:k + 1], None,
                    ALU.mult), inc="vs" if k == 15 else None)
            wout_ready_vs = b.c["vs"]

            b.do("vector", lambda v: v.memset(pmisc[:, 0:128], 0.0))
            b.do("vector", lambda v: v.memset(cst[:, 0:1], EPS_LN))
            b.do("vector", lambda v: v.memset(cst[:, 1:2], EPS_RMS))
            b.do("vector", lambda v: v.memset(state[:], 0.0))
            b.do("vector", lambda v: v.memset(ln2T[0][:], 0.0))
            b.do("vector", lambda v: v.memset(ln2T[1][:], 0.0), inc="vs")

            x_load_q = [0] * (2 * NT)
            x_store_q = [0] * (2 * NT)
            state_evac_vs = 0

            st_ln1T_ts = [0]
            lf_state = {"ln1_vs": [0, 0], "factors_vs": 0}

            def emit_lf(tF, tB):
                xbf = xt[tF % 3] if tF is not None else None
                c0 = 4 * (tF % 2) if tF is not None else 0
                if tF is not None and x_load_q[tF] == 0:
                    b.wait("sync", "vs")
                    x_load_q[tF] = b.do("sync", lambda q, tF=tF, xbf=xbf:
                        q.dma_start(xbf[:], x_h[tF * 128:(tF + 1) * 128, :]),
                        inc="qx")

                if tF is not None:
                    b.wait("vector", "qx", x_load_q[tF])
                    b.do("vector", lambda v: v.scalar_tensor_tensor(
                        scrV[:, 0:DM], xbf[:], 1.0 / DM, xbf[:], ALU.mult,
                        ALU.mult, accum_out=stat[:, c0:c0 + 1]), inc="vs")
                vfence(b)
                if tB is not None:
                    b.do("vector", lambda v: v.tensor_tensor_scan(
                        lp[:], e16[:], e16[:], 0.0, ALU.add, ALU.bypass))
                    vfence(b)
                    b.do("vector", lambda v: v.tensor_scalar(
                        sv16[:, 0:1], lp[:, 63:64], -1.0, None, ALU.mult))
                    b.do("vector", lambda v: v.tensor_scalar(
                        sv16[:, 1:2], lp[:, 127:128], -1.0, None, ALU.mult),
                        inc="vs")
                if tF is not None:
                    b.wait("scalar", "qx", x_load_q[tF])
                    b.wait("scalar", "vs")
                    b.do("scalar", lambda s: s.activation(
                        scrS[:, 0:DM], xbf[:], AF.Identity, scale=1.0 / DM,
                        accum_out=stat[:, c0 + 1:c0 + 2]))
                    sfence(b)
                    b.wait("vector", "ss")
                    b.do("vector", lambda v: v.scalar_tensor_tensor(
                        stat[:, c0 + 2:c0 + 3], stat[:, c0 + 1:c0 + 2], stat[:, c0 + 1:c0 + 2],
                        stat[:, c0:c0 + 1], ALU.mult, ALU.subtract), inc="vs")
                b.wait("scalar", "vs")
                if tF is not None:
                    b.do("scalar", lambda s: s.activation(
                        stat[:, c0 + 3:c0 + 4], stat[:, c0 + 2:c0 + 3], AF.Ln, scale=-1.0,
                        bias=cst[:, 0:1]))
                    b.do("scalar", lambda s: s.activation(
                        stat[:, c0 + 3:c0 + 4], stat[:, c0 + 3:c0 + 4], AF.Exp, scale=-0.5),
                        inc="ss" if tB is None else None)
                if tB is not None:
                    b.do("scalar", lambda s: s.activation(
                        exsc[:], lp[:], AF.Exp, scale=-1.0))
                    b.do("scalar", lambda s: s.activation(
                        exs2[:, 0:64], lp[:, 64:128], AF.Exp, scale=-1.0,
                        bias=lp[:, 63:64]))
                    b.do("scalar", lambda s: s.activation(
                        exs2[:, 64:128], lp[:, 0:64], AF.Exp))
                    b.do("scalar", lambda s: s.activation(
                        exs4[:, 0:64], lp[:, 64:128], AF.Exp,
                        bias=sv16[:, 0:1]))
                    b.do("scalar", lambda s: s.activation(
                        exs3[:], lp[:], AF.Exp, bias=sv16[:, 1:2]))
                    b.do("scalar", lambda s: s.activation(
                        sv16[:, 2:3], sv16[:, 0:1], AF.Exp))
                    b.do("scalar", lambda s: s.activation(
                        sv16[:, 3:4], sv16[:, 1:2], AF.Exp), inc="ss")
                b.wait("vector", "ss")
                if tF is not None:
                    b.wait("vector", "ts", st_ln1T_ts[0])
                    b.do("vector", lambda v, ln1=ln1b[tF % 2]: v.tensor_scalar(
                        ln1[:], xbf[:], stat[:, c0 + 1:c0 + 2], stat[:, c0 + 3:c0 + 4],
                        ALU.subtract, ALU.mult),
                        inc="vs" if tB is None else None)
                if tB is not None:
                    b.wait("vector", "qc", consts_qs)
                    B_ap = abc[:, 128:256]
                    C_ap = abc[:, 256:384]
                    b.do("vector", lambda v: v.tensor_tensor(
                        cint[:], exsc[:], C_ap, ALU.mult))
                    b.do("vector", lambda v: v.tensor_tensor(
                        cex2[:], exs2[:, 0:64], C_ap[:, 64:128], ALU.mult))
                    b.do("vector", lambda v: v.tensor_tensor(
                        bex1[:], exs2[:, 64:128], B_ap[:, 0:64], ALU.mult))
                    b.do("vector", lambda v: v.tensor_tensor(
                        bex2[:], exs4[:, 0:64], B_ap[:, 64:128], ALU.mult))
                    b.do("vector", lambda v: v.tensor_scalar(
                        bex1x[:], bex1[:], sv16[:, 2:3], None, ALU.mult))
                    b.do("vector", lambda v: v.tensor_tensor(
                        bsc[:], exs3[:], B_ap, ALU.mult))
                    b.do("vector", lambda v: v.tensor_scalar(
                        dgl[:], i16[:], sv16[:, 3:4], None, ALU.mult),
                        inc="vs")
                if False:
                    dump(b, 6, lp[:], 16, 128)
                    dump(b, 7, cint[:], 16, 128)
                    dump(b, 8, bex1[:], 16, 64)

                lf_state["factors_vs"] = b.c["vs"]
                if tF is not None:
                    lf_state["ln1_vs"][tF % 2] = b.c["vs"]

            emit_lf(0, None)
            emit_lf(1, None)
            for it in range(NT + 1):
                t = it
                tb = it - 1
                F = t < NT
                B = tb >= 0
                pf = t % 2
                pb = tb % 2
                xif = x_in[pf]
                xib = x_in[pb]
                xbf = xt[t % 3] if F else None
                xbb = xt[tb % 3] if B else None
                factors_vs = lf_state["factors_vs"]
                ln1_vs = lf_state["ln1_vs"][t % 2]

                if F:
                    b.wait("tensor", "qc", consts_qs)
                    transpose_group(b, ln1T, ln1b[t % 2], DM, ln1_vs)
                    st_ln1T_ts[0] = b.c["ts"]
                    rec["ln1T_vs"] = b.c["vs"]

                def xz_pass(jps):
                    b.wait("tensor", "vs",
                           max(rec["ln1T_vs"], rec["gate_vs"]))
                    b.wait("tensor", "ss", rec["gate_ss"])
                    b.wait("tensor", "w1", 16)
                    for k in range(8):
                        for half, pyb in ((0, py0), (1, py1)):
                            jp = jps[half]
                            b.do("tensor", lambda tt, k=k, jp=jp, pyb=pyb:
                                 tt.matmul(pyb[:, 0:512],
                                           ln1T[:, k * 128:(k + 1) * 128],
                                           win[:, k, jp * 512:(jp + 1) * 512],
                                           start=(k == 0), stop=(k == 7)))
                            b.do("tensor", lambda tt, k=k, jp=jp, pyb=pyb:
                                 tt.matmul(pyb[:, 512:1024],
                                           ln1T[:, k * 128:(k + 1) * 128],
                                           win[:, k, 2048 + jp * 512:
                                               2048 + (jp + 1) * 512],
                                           start=(k == 0), stop=(k == 7)),
                                 inc="ts" if (half == 1 and k == 7) else None)
                    b.wait("scalar", "ts")
                    for half, pyb, sa, sbuf_ in ((0, py0, s1b, s2b),
                                                 (1, py1, s1b2, s2b2)):
                        b.do("scalar", lambda s, pyb=pyb, sa=sa: s.activation(
                            sa[:], pyb[:, 0:512], AF.Sigmoid))
                        b.do("scalar", lambda s, pyb=pyb, sbuf_=sbuf_:
                             s.activation(sbuf_[:], pyb[:, 512:1024],
                                          AF.Sigmoid),
                             inc="ss" if half == 1 else None)
                    b.wait("vector", "ss")
                    for half, pyb, sa, sbuf_ in ((0, py0, s1b, s2b),
                                                 (1, py1, s1b2, s2b2)):
                        jp = jps[half]
                        b.do("vector", lambda v, sa=sa, sbuf_=sbuf_:
                             v.tensor_tensor(sa[:], sa[:], sbuf_[:], ALU.mult))
                        b.do("vector", lambda v, jp=jp, pyb=pyb, sa=sa:
                             v.tensor_tensor(xif[:, jp * 512:(jp + 1) * 512],
                                             pyb[:, 0:512], sa[:], ALU.mult),
                             inc="vs" if half == 1 else None)
                    rec["gate_vs"] = b.c["vs"]
                    rec["gate_ss"] = b.c["ss"]
                if F:
                    xz_pass((0, 1))

                if B:
                    b.wait("tensor", "vs", max(factors_vs, rec["ptp_vs"],
                                               rec["abc_tm_vs"], rec["mt_vs"]))
                    b.wait("tensor", "qc", consts_qs)
                    b.do("tensor", lambda tt: tt.matmul(
                        pmisc[0:64, 0:64], bex1[:], cint[:, 0:64],
                        start=True, stop=True))
                    b.do("tensor", lambda tt: tt.matmul(
                        pmisc[0:64, 64:128], bex1x[:], cex2[:],
                        start=True, stop=True))
                    b.do("tensor", lambda tt: tt.matmul(
                        pmisc[64:128, 64:128], bex2[:], cex2[:],
                        start=True, stop=True, tile_position=(0, 64)))
                    b.do("tensor", lambda tt: tt.transpose(
                        ptp[:, 0:16], bsc[:], i16[:]), inc="ts")
                    b.wait("vector", "ts")
                    b.wait("vector", "qc", consts_qs)
                    b.do("vector", lambda v: v.tensor_tensor(
                        mt[:], pmisc[:, 0:128], mmask[:], ALU.mult))
                    b.do("vector", lambda v: v.tensor_copy(
                        bscT[:], ptp[:, 0:16]), inc="vs")
                    mt_vs = b.c["vs"]
                    rec["mt_vs"] = mt_vs
                    rec["ptp_vs"] = mt_vs
                if B and tb == 0:
                    dump(b, 9, mt[:], 128, 128)


                if B:
                    b.wait("tensor", "vs", max(mt_vs, state_evac_vs,
                                               rec["postln_vs"], rec["bc_vs"]))
                    b.wait("tensor", "ss", max(rec["postln_ss"], rec["ez_ss"]))
                    regs = [(po, 0), (po, 512), (pmisc, 0)]
                    ssq_cols = [8, 9, 10, 11]
                    out_evac_vs = {}
                    for dc in range(3):
                        reg, roff = regs[dc]
                        b.do("tensor", lambda tt, reg=reg, roff=roff, dc=dc:
                             tt.matmul(reg[:, roff:roff + 512], cint[:],
                                       state[:, dc * 512:(dc + 1) * 512],
                                       start=True, stop=False))
                        b.do("tensor", lambda tt, reg=reg, roff=roff, dc=dc:
                             tt.matmul(reg[:, roff:roff + 512], mt[:],
                                       xib[:, dc * 512:(dc + 1) * 512],
                                       start=False, stop=True), inc="ts")
                        b.wait("vector", "ts")
                        out_evac_vs[dc] = b.do("vector",
                            lambda v, reg=reg, roff=roff, dc=dc: v.tensor_copy(
                                out_n[:, dc * 512:(dc + 1) * 512],
                                reg[:, roff:roff + 512]), inc="vs")
                        b.wait("scalar", "vs", out_evac_vs[dc])
                        b.do("scalar", lambda s, dc=dc, c=ssq_cols[dc]:
                             s.activation(scrS[:, 0:512],
                                          out_n[:, dc * 512:(dc + 1) * 512],
                                          AF.Square, scale=SQDI,
                                          accum_out=stat[:, c:c + 1]),
                             inc="ss")

                if F:
                    xz_pass((2, 3))
                    gating_vs = b.c["vs"]

                if B:
                    b.wait("tensor", "vs", out_evac_vs[0])
                    b.do("tensor", lambda tt: tt.matmul(
                        po[:, 0:512], cint[:], state[:, 1536:2048],
                        start=True, stop=False))
                    b.do("tensor", lambda tt: tt.matmul(
                        po[:, 0:512], mt[:], xib[:, 1536:2048],
                        start=False, stop=True), inc="ts")
                    b.wait("vector", "ts")
                    out_evac_vs[3] = b.do("vector", lambda v: v.tensor_copy(
                        out_n[:, 1536:2048], po[:, 0:512]), inc="vs")
                    b.wait("scalar", "vs", out_evac_vs[3])
                    b.do("scalar", lambda s: s.activation(
                        scrS[:, 0:512], out_n[:, 1536:2048], AF.Square,
                        scale=SQDI, accum_out=stat[:, 11:12]))
                    sfence(b)

                if B:
                    b.wait("tensor", "ss", rec["ez_ss"])
                    sregs = [(pmisc, 0, out_evac_vs[2]),
                             (po, 512, out_evac_vs[1]),
                             (po, 0, out_evac_vs[3])]
                    st_evac = {}
                    for dc in range(3):
                        reg, roff, wvv = sregs[dc]
                        b.wait("tensor", "vs", wvv)
                        b.do("tensor", lambda tt, reg=reg, roff=roff, dc=dc:
                             tt.matmul(reg[0:16, roff:roff + 512], dgl[:],
                                       state[:, dc * 512:(dc + 1) * 512],
                                       start=True, stop=False))
                        b.do("tensor", lambda tt, reg=reg, roff=roff, dc=dc:
                             tt.matmul(reg[0:16, roff:roff + 512], bscT[:],
                                       xib[:, dc * 512:(dc + 1) * 512],
                                       start=False, stop=True), inc="ts")
                        b.wait("vector", "ts")
                        st_evac[dc] = b.do("vector",
                            lambda v, reg=reg, roff=roff, dc=dc: v.tensor_copy(
                                state[:, dc * 512:(dc + 1) * 512],
                                reg[0:16, roff:roff + 512]), inc="vs")

                if F:
                    transpose_group(b, x_inT, xif, DI, rec["gate_vs"])
                    x_inT_vs = b.c["vs"]
                if F and t == 0:
                    dump(b, 1, xif[:], 128, DI)
                    dump(b, 2, x_inT[:], 128, DI)

                if B:
                    b.wait("tensor", "vs", st_evac[0])
                    b.do("tensor", lambda tt: tt.matmul(
                        pmisc[0:16, 0:512], dgl[:], state[:, 1536:2048],
                        start=True, stop=False))
                    b.do("tensor", lambda tt: tt.matmul(
                        pmisc[0:16, 0:512], bscT[:], xib[:, 1536:2048],
                        start=False, stop=True), inc="ts")
                    b.wait("vector", "ts")
                    st_evac[3] = b.do("vector", lambda v: v.tensor_copy(
                        state[:, 1536:2048], pmisc[0:16, 0:512]), inc="vs")
                    state_evac_vs = st_evac[3]

                if B:
                    b.wait("vector", "ss")
                    b.do("vector", lambda v: v.tensor_reduce(
                        stat[:, 24:25], stat[:, 8:12], mybir.AxisListType.X,
                        ALU.add), inc="vs")
                    b.wait("scalar", "vs")
                    b.do("scalar", lambda s: s.activation(
                        stat[:, 25:26], stat[:, 24:25], AF.Ln,
                        scale=1.0, bias=cst[:, 1:2]))
                    b.do("scalar", lambda s: s.activation(
                        stat[:, 25:26], stat[:, 25:26], AF.Exp, scale=-0.5),
                        inc="ss")
                    if DBG_DUMP and tb == 0:
                        b.wait("gpsimd", "vs")
                        b.do("gpsimd", lambda g: g.dma_start(
                            dbg_h[2, :, 0:DI], out_n[:]), inc="gs")
                        b.wait("vector", "gs")
                    b.wait("vector", "ss")
                    b.do("vector", lambda v: v.tensor_scalar(
                        out_n[:], out_n[:], stat[:, 25:26], None, ALU.mult),
                        inc="vs")
                    rec["rms_vs"] = b.c["vs"]
                if B and tb == 0:
                    dump(b, 10, out_n[:], 128, DI)
                    dump(b, 11, stat[:], 128, 32)

                if F:
                    b.wait("tensor", "vs", x_inT_vs)
                    if B:
                        b.wait("tensor", "vs", st_evac[3])
                    b.wait("tensor", "w2", 16)
                    for k in range(16):
                        b.do("tensor", lambda tt, k=k: tt.matmul(
                            pmisc[:, 0:48], x_inT[:, k * 128:(k + 1) * 128],
                            wabc[:, k, :], start=(k == 0), stop=(k == 15)),
                            inc="ts" if k == 15 else None)
                    b.wait("vector", "ts")
                    b.do("vector", lambda v: v.tensor_copy(
                        abc_tm[:], pmisc[:, 0:48]), inc="vs")
                    abc_tm_vs = b.c["vs"]
                    rec["abc_tm_vs"] = abc_tm_vs
                if F and t == 0:
                    dump(b, 3, abc_tm[:], 128, 48)

                if B:
                    transpose_group(b, out_nT, out_n, DI, rec["rms_vs"])
                    out_nT_vs = b.c["vs"]

                if F:
                    b.wait("tensor", "vs", max(abc_tm_vs, rec["bc_vs"]))
                    b.wait("tensor", "ss", rec["ez_ss"])
                    b.wait("tensor", "qc", consts_qs)
                    for j in range(3):
                        b.do("tensor", lambda tt, j=j: tt.transpose(
                            pmisc[0:16, 128 + j * 128:256 + j * 128],
                            abc_tm[:, j * 16:(j + 1) * 16], i128f[:]),
                            inc="ts" if j == 2 else None)
                    b.wait("scalar", "ts")
                    b.do("scalar", lambda s: s.activation(
                        e16[:], pmisc[0:16, 128:256], AF.Exp), inc="ss")
                    b.wait("vector", "ts")
                    b.wait("vector", "ss")
                    b.do("vector", lambda v: v.tensor_copy(
                        abc[:, 128:256], pmisc[0:16, 256:384]))
                    b.do("vector", lambda v: v.tensor_copy(
                        abc[:, 256:384], pmisc[0:16, 384:512]))
                    b.do("vector", lambda v: v.tensor_scalar(
                        e16[:], e16[:], EM5, 1.0, ALU.max, ALU.min),
                        inc="vs")
                    rec["bc_vs"] = b.c["vs"]
                    rec["ez_ss"] = b.c["ss"]
                if F and t >= 1 and t + 1 < NT:
                    emit_lf(t + 1, None)
                if F and t == 0:
                    dump(b, 4, e16[:], 16, 128)
                    dump(b, 5, abc[:, 128:384], 16, 256)

                if B:
                    b.wait("tensor", "vs", out_nT_vs)
                    b.wait("tensor", "vs", wout_ready_vs)
                    for k in range(16):
                        for mc in range(2):
                            b.do("tensor", lambda tt, mc=mc, k=k: tt.matmul(
                                po[:, mc * 512:(mc + 1) * 512],
                                out_nT[:, k * 128:(k + 1) * 128],
                                wout[:, k, mc * 512:(mc + 1) * 512],
                                start=(k == 0), stop=(k == 15)),
                                inc="ts" if (mc == 1 and k == 15) else None)
                    b.wait("scalar", "ts")
                    b.do("scalar", lambda s: s.activation(
                        scrS[:, 0:1024], po[:], AF.Square, scale=SQDM,
                        accum_out=stat[:, 12:13]))
                    b.do("scalar", lambda s: s.activation(
                        scrS[:, 0:1024], po[:], AF.Identity, scale=1.0 / DM,
                        accum_out=stat[:, 13:14]))
                    sfence(b)
                    b.wait("vector", "ss")
                    b.do("vector", lambda v: v.scalar_tensor_tensor(
                        stat[:, 14:15], stat[:, 13:14], stat[:, 13:14],
                        stat[:, 12:13], ALU.mult, ALU.subtract), inc="vs")
                    b.wait("scalar", "vs")
                    b.do("scalar", lambda s: s.activation(
                        stat[:, 15:16], stat[:, 14:15], AF.Ln, scale=-1.0,
                        bias=cst[:, 0:1]))
                    b.do("scalar", lambda s: s.activation(
                        stat[:, 15:16], stat[:, 15:16], AF.Exp, scale=-0.5),
                        inc="ss")
                    rec["postln_ss"] = b.c["ss"]
                    b.wait("vector", "ss")
                    b.wait("vector", "qy", x_store_q[tb - 1] if tb >= 1 else 0)
                    b.do("vector", lambda v: v.tensor_scalar(
                        tscr[:], po[:], stat[:, 13:14], stat[:, 15:16],
                        ALU.subtract, ALU.mult))
                    b.do("vector", lambda v, xbb=xbb: v.tensor_tensor(
                        tscr[:], tscr[:], xbb[:], ALU.add), inc="vs")
                    rec["postln_vs"] = b.c["vs"]
                    b.wait("sync", "vs")
                    x_store_q[tb] = b.do("sync", lambda q, tb=tb:
                        q.dma_start(y_h[tb * 128:(tb + 1) * 128, :], tscr[:]),
                        inc="qy")

                emit_lf(None, t if t < NT else None)
                if t + 2 < NT and x_load_q[t + 2] == 0:
                    b.wait("sync", "vs")
                    x_load_q[t + 2] = b.do("sync", lambda q, t=t:
                        q.dma_start(xt[(t + 2) % 3][:],
                                    x_h[(t + 2) * 128:(t + 3) * 128, :]),
                        inc="qx")

            phaseA_done_q = b.c["qy"]

            for it in range(NT + 1):
                u = it
                ub = it - 1
                F = u < NT
                B = ub >= 0
                xbf = xt[u % 3] if F else None
                xbb = xt[ub % 3] if B else None

                if F:
                    b.wait("sync", "qy", phaseA_done_q)
                    if u >= 2:
                        b.wait("sync", "qy", x_store_q[NT + u - 2])
                    x_load_q[NT + u] = b.do("sync", lambda q, u=u, xbf=xbf:
                        q.dma_start(xbf[:], y_h[u * 128:(u + 1) * 128, :]),
                        inc="qx")

                if B:
                    b.wait("tensor", "vs", max(rec["mix_vs"], rec["kact_vs"]))
                    b.wait("tensor", "ss", rec["rb_ss"])
                    b.wait("tensor", "w4", 16)
                    b.wait("tensor", "w5", 16)
                    for k in range(8):
                        for mc in range(2):
                            b.do("tensor", lambda tt, mc=mc, k=k: tt.matmul(
                                py0[:, mc * 512:(mc + 1) * 512],
                                xrT[:, k * 128:(k + 1) * 128],
                                wr[:, k, mc * 512:(mc + 1) * 512],
                                start=(k == 0), stop=(k == 7)))
                    for k in range(8):
                        for mc in range(2):
                            b.do("tensor", lambda tt, mc=mc, k=k: tt.matmul(
                                py1[:, mc * 512:(mc + 1) * 512],
                                xkT[:, k * 128:(k + 1) * 128],
                                wk[:, k, mc * 512:(mc + 1) * 512],
                                start=(k == 0), stop=(k == 7)),
                                inc="ts" if (mc == 1 and k == 7) else None)
                    rk_ts = b.c["ts"]

                ln2 = ln1b[u % 2] if F else None
                if F:
                    ln_stats_apply(b, xbf[:], DM, ln2[:], 0, cst[:, 0:1],
                                   x_load_q[NT + u])
                    rec["ln2_vs"] = b.c["vs"]

                if B:
                    b.wait("scalar", "ts", rk_ts)
                    b.do("scalar", lambda s: s.activation(
                        r_b, py0[:], AF.Sigmoid), inc="ss")
                    rec["rb_ss"] = b.c["ss"]
                    b.wait("vector", "ts", rk_ts)
                    b.do("vector", lambda v: v.tensor_scalar(
                        tscr[:], py1[:], 0.0, None, ALU.max))
                    b.do("vector", lambda v: v.tensor_tensor(
                        kact, tscr[:], tscr[:], ALU.mult), inc="vs")
                    kact_vs = b.c["vs"]
                    rec["kact_vs"] = kact_vs

                if F:
                    par = u % 2
                    lnb = ln2T[par]
                    lnprev = ln2T[1 - par]
                    for g in range(2):
                        b.wait("tensor", "vs", max(rec["ln2_vs"], rec["ptp_vs"]))
                        b.wait("tensor", "qc", consts_qs)
                        for i in range(4):
                            k = g * 4 + i
                            b.do("tensor", lambda tt, k=k, i=i: tt.transpose(
                                ptp[:, i * 128:(i + 1) * 128],
                                ln2[:, k * 128:(k + 1) * 128], i128[:]),
                                inc="ts" if i == 3 else None)
                        b.wait("vector", "ts")
                        for i in range(4):
                            k = g * 4 + i
                            b.do("vector", lambda v, k=k, i=i: v.tensor_copy(
                                lnb[:, k, 1:129],
                                ptp[:, i * 128:(i + 1) * 128]),
                                inc="vs" if i == 3 else None)
                        rec["ptp_vs"] = b.c["vs"]
                    if u > 0:
                        for k in range(8):
                            b.do("vector", lambda v, k=k: v.tensor_copy(
                                lnb[:, k, 0:1], lnprev[:, k, 128:129]),
                                inc="vs" if k == 7 else None)
                    b.wait("vector", "qc", consts_qs)
                    for k in range(8):
                        b.do("vector", lambda v, k=k: v.tensor_tensor(
                            dmix[:], lnb[:, k, 1:129], lnb[:, k, 0:128],
                            ALU.subtract))
                        b.do("vector", lambda v, k=k: v.scalar_tensor_tensor(
                            xrT[:, k * 128:(k + 1) * 128], dmix[:],
                            mur[:, k:k + 1], lnb[:, k, 0:128],
                            ALU.mult, ALU.add))
                        b.do("vector", lambda v, k=k: v.scalar_tensor_tensor(
                            xkT[:, k * 128:(k + 1) * 128], dmix[:],
                            muk[:, k:k + 1], lnb[:, k, 0:128],
                            ALU.mult, ALU.add), inc="vs" if k == 7 else None)
                    rec["mix_vs"] = b.c["vs"]

                if B:
                    b.wait("tensor", "vs", kact_vs)
                    transpose_group(b, kactT, kact, DM, rec["kact_vs"])
                    kactT_vs = b.c["vs"]
                    b.wait("tensor", "vs", max(kactT_vs, rec["fout_vs"]))
                    b.wait("tensor", "ss", rec["rb_ss"])
                    b.wait("tensor", "w6", 16)
                    for k in range(8):
                        for mc in range(2):
                            b.do("tensor", lambda tt, mc=mc, k=k: tt.matmul(
                                po[:, mc * 512:(mc + 1) * 512],
                                kactT[:, k * 128:(k + 1) * 128],
                                wv[:, k, mc * 512:(mc + 1) * 512],
                                start=(k == 0), stop=(k == 7)),
                                inc="ts" if (mc == 1 and k == 7) else None)
                    b.wait("vector", "ts")
                    b.wait("vector", "qy",
                           x_store_q[NT + ub - 1] if ub >= 1 else phaseA_done_q)
                    b.do("vector", lambda v: v.tensor_tensor(
                        fout[:], po[:], r_b, ALU.mult), inc="vs")
                    rec["fout_vs"] = b.c["vs"]
                    if DBG_DUMP and ub == 0:
                        dump(b, 2, xbb[:], 128, DM)
                        dump(b, 0, fout[:], 128, DM)
                        dump(b, 1, r_b, 128, DM)
                        dump(b, 4, kact, 128, DM)
                        dump(b, 5, xrT[:, 0:DM], 128, DM)
                        dump(b, 6, xkT[:, 0:DM], 128, DM)
                        dump(b, 7, ln2[:], 128, DM)
                    b.do("vector", lambda v: v.scalar_tensor_tensor(
                        scrV[:, 0:1024], fout[:], 1.0 / DM, fout[:],
                        ALU.mult, ALU.mult, accum_out=stat[:, 16:17]))
                    vfence(b)
                    b.wait("scalar", "vs")
                    b.do("scalar", lambda s: s.activation(
                        scrS[:, 0:1024], fout[:], AF.Identity,
                        scale=1.0 / DM, accum_out=stat[:, 17:18]))
                    sfence(b)
                    b.wait("vector", "ss")
                    b.do("vector", lambda v: v.scalar_tensor_tensor(
                        stat[:, 18:19], stat[:, 17:18], stat[:, 17:18],
                        stat[:, 16:17], ALU.mult, ALU.subtract), inc="vs")
                    b.wait("scalar", "vs")
                    b.do("scalar", lambda s: s.activation(
                        stat[:, 19:20], stat[:, 18:19], AF.Ln, scale=-1.0,
                        bias=cst[:, 0:1]))
                    b.do("scalar", lambda s: s.activation(
                        stat[:, 19:20], stat[:, 19:20], AF.Exp, scale=-0.5),
                        inc="ss")
                    b.wait("vector", "ss")
                    b.do("vector", lambda v: v.tensor_scalar(
                        fout[:], fout[:], stat[:, 17:18], stat[:, 19:20],
                        ALU.subtract, ALU.mult))
                    b.do("vector", lambda v, xbb=xbb: v.tensor_tensor(
                        xbb[:], fout[:], xbb[:], ALU.add), inc="vs")
                    if DBG_DUMP and ub == 0:
                        dump(b, 8, stat[:], 128, 32)
                        dump(b, 3, xbb[:], 128, DM)
                    b.wait("sync", "vs")
                    x_store_q[NT + ub] = b.do("sync", lambda q, ub=ub, xbb=xbb:
                        q.dma_start(y_h[ub * 128:(ub + 1) * 128, :], xbb[:]),
                        inc="qy")

            b.wait("sync", "qx", b.c["qx"])
            b.wait("sync", "qy", b.c["qy"])
            for _wn in ("w1", "w2", "w3", "w4", "w5", "w6"):
                b.wait("gpsimd", _wn, 16)

        @blk.sync
        def _(sync):
            run(Pass("sync", sync, sems))

        @blk.gpsimd
        def _(g):
            run(Pass("gpsimd", g, sems))

        @blk.tensor
        def _(tt):
            run(Pass("tensor", tt, sems))

        @blk.vector
        def _(v):
            run(Pass("vector", v, sems))

        @blk.scalar
        def _(s):
            run(Pass("scalar", s, sems))

    return nc


_NC_CACHE = None


def kernel(L0_attn_x, L0_attn_W_in, L0_attn_W_A, L0_attn_W_B, L0_attn_W_C,
           L0_attn_W_out, L0_attn_norm_w, L0_ffn_W_r, L0_ffn_W_k, L0_ffn_W_v,
           L0_ffn_mu_r, L0_ffn_mu_k):
    global _NC_CACHE
    x = np.ascontiguousarray(np.asarray(L0_attn_x, dtype=np.float32))
    w_in = np.ascontiguousarray(np.asarray(L0_attn_W_in, dtype=np.float32))
    w_abc = np.ascontiguousarray(np.concatenate(
        [np.asarray(L0_attn_W_A, dtype=np.float32),
         np.asarray(L0_attn_W_B, dtype=np.float32),
         np.asarray(L0_attn_W_C, dtype=np.float32)], axis=1))
    w_out = np.ascontiguousarray(np.asarray(L0_attn_W_out, dtype=np.float32))
    nw = np.ascontiguousarray(
        np.asarray(L0_attn_norm_w, dtype=np.float32).reshape(16, 128).T)
    w_rt = np.ascontiguousarray(np.asarray(L0_ffn_W_r, dtype=np.float32).T)
    w_kt = np.ascontiguousarray(np.asarray(L0_ffn_W_k, dtype=np.float32).T)
    w_vt = np.ascontiguousarray(np.asarray(L0_ffn_W_v, dtype=np.float32).T)
    mur = np.ascontiguousarray(
        np.asarray(L0_ffn_mu_r, dtype=np.float32).reshape(8, 128).T)
    muk = np.ascontiguousarray(
        np.asarray(L0_ffn_mu_k, dtype=np.float32).reshape(8, 128).T)
    i128 = np.eye(128, dtype=np.float32).astype(bfnp)
    i128f = np.eye(128, dtype=np.float32)
    i16 = np.eye(16, dtype=np.float32).astype(bfnp)
    H = 64
    mm = np.zeros((128, 128), np.float32)
    mm[:H, :H] = np.triu(np.ones((H, H)))
    mm[H:, H:] = np.triu(np.ones((H, H)))
    mm[:H, H:] = 1.0
    mmask = mm.astype(bfnp)

    if _NC_CACHE is None:
        _NC_CACHE = build_nc()
    nc = _NC_CACHE

    shared = dict(w_in=w_in, w_abc=w_abc, w_out=w_out, nw=nw, w_rt=w_rt,
                  w_kt=w_kt, w_vt=w_vt, mur=mur, muk=muk, i128=i128, i128f=i128f,
                  i16=i16, mmask=mmask)
    in_maps = [dict(x=np.ascontiguousarray(x[c]), **shared) for c in range(8)]
    res = run_bass_kernel_spmd(nc, in_maps, core_ids=list(range(8)))
    out = np.stack([np.asarray(r["y"]) for r in res.results], axis=0)
    return out.astype(np.float32)
